# version 09
# baseline (speedup 1.0000x reference)
from contextlib import ExitStack

import numpy as np
import concourse.bass as bass
import concourse.mybir as mybir
from concourse.bass_utils import run_bass_kernel_spmd

F32 = mybir.dt.float32
BF16 = mybir.dt.bfloat16
AF = mybir.ActivationFunctionType
ALU = mybir.AluOpType
AX = mybir.AxisListType.X

S = 4096
D = 1024
NT = S // 128
NC = D // 128
HID = 2730
HIDP = 2816
FF = 5460
NHC = 22
LN8 = float(np.log(8.0))
NEGBIG = -1.0e9
ENGS = ("tensor", "scalar", "vector", "gpsimd", "sync")


class Prog:
    def __init__(self, nc):
        self.nc = nc
        self.ops = {e: [] for e in ENGS}
        self.sem = {}
        self.cnt = {}

    def mksem(self, name):
        self.sem[name] = self.nc.alloc_semaphore(name)
        self.cnt[name] = 0

    def op(self, eng, fn, waits=(), inc=None, inck=1):
        self.ops[eng].append((fn, tuple(waits), inc, inck))
        if inc is not None:
            self.cnt[inc] += inck
            return self.cnt[inc]
        return None

    def w(self, name):
        return (self.sem[name], self.cnt[name])

    def snap(self):
        return {n: self.cnt[n] for n in self.sem}

    def barrier(self):
        waits = [(self.sem[n], self.cnt[n]) for n in self.sem]
        used = {e: False for e in ENGS}

        def first(eng, extra=()):
            if not used[eng]:
                used[eng] = True
                return list(waits) + list(extra)
            return list(extra)
        return first

    def emit(self):
        with self.nc.Block() as blk:
            for eng in ENGS:
                if not self.ops[eng]:
                    continue

                def body(e, _ops=tuple(self.ops[eng])):
                    for fn, waits, inc, inck in _ops:
                        for sem, val in waits:
                            if val > 0:
                                e.wait_ge(sem, val)
                        ins = fn(e) if fn is not None else None
                        if inc is not None:
                            ins.then_inc(self.sem[inc], inck)

                getattr(blk, eng)(body)


def build_nc(debug=False):
    nc = bass.Bass("TRN2", target_bir_lowering=False)
    p = Prog(nc)
    for s_ in ("qWG", "qHG", "qL", "qY", "qZ", "qS1", "qS2", "qS3", "qS4", "qD",
               "sP", "sV", "sC", "sL"):
        p.mksem(s_)
    SEM = p.sem

    d_x = nc.dram_tensor("x", [S, D], F32, kind="ExternalInput")
    d_wq = nc.dram_tensor("wq", [D, D], F32, kind="ExternalInput")
    d_wk = nc.dram_tensor("wk", [D, D], F32, kind="ExternalInput")
    d_wv = nc.dram_tensor("wv", [D, D], F32, kind="ExternalInput")
    d_iwqkv = [nc.dram_tensor(f"iwqkv{l}", [D, 1536], F32, kind="ExternalInput") for l in range(2)]
    d_iwo = [nc.dram_tensor(f"iwo{l}", [512, D], F32, kind="ExternalInput") for l in range(2)]
    d_iw1 = [nc.dram_tensor(f"iw1{l}", [D, FF], F32, kind="ExternalInput") for l in range(2)]
    d_iw2 = [nc.dram_tensor(f"iw2{l}", [HIDP, D], F32, kind="ExternalInput") for l in range(2)]
    d_qs = [nc.dram_tensor(f"qs{l}", [1, 64], F32, kind="ExternalInput") for l in range(2)]
    d_ks = [nc.dram_tensor(f"ks{l}", [1, 64], F32, kind="ExternalInput") for l in range(2)]
    d_g = [nc.dram_tensor(f"g{l}", [128, 8], F32, kind="ExternalInput") for l in range(2)]
    d_b = [nc.dram_tensor(f"bb{l}", [128, 8], F32, kind="ExternalInput") for l in range(2)]
    d_b1a = [nc.dram_tensor(f"b1a{l}", [128, 22], F32, kind="ExternalInput") for l in range(2)]
    d_b1g = [nc.dram_tensor(f"b1g{l}", [128, 22], F32, kind="ExternalInput") for l in range(2)]
    d_b2 = [nc.dram_tensor(f"b2{l}", [128, 8], F32, kind="ExternalInput") for l in range(2)]
    d_tab = [nc.dram_tensor(nm, [S, 64], F32, kind="ExternalInput")
             for nm in ("tCS", "tSS", "tCiS", "tSiS")]
    d_out = nc.dram_tensor("out", [S, D], F32, kind="ExternalOutput")

    kd = "ExternalOutput" if debug else "Internal"
    if debug:
        d_dbgQ = nc.dram_tensor("dbgQ", [128, S], BF16, kind="ExternalOutput")
        d_dbgK = nc.dram_tensor("dbgK", [128, S], BF16, kind="ExternalOutput")
        d_dbgV = nc.dram_tensor("dbgV", [128, NT * 132], BF16, kind="ExternalOutput")
        d_dbgM = nc.dram_tensor("dbgM", [128, 2], F32, kind="ExternalOutput")
        d_dbgP = nc.dram_tensor("dbgP", [128, 1024], BF16, kind="ExternalOutput")
        d_dbgZ = nc.dram_tensor("dbgZ", [65, 1024], F32, kind="ExternalOutput")
        d_dbgXN = nc.dram_tensor("dbgXN", [128, NC * S], BF16, kind="ExternalOutput")
        d_dbgXNT = nc.dram_tensor("dbgXNT", [S, D], F32, kind="ExternalOutput")
        d_dbgZT2 = nc.dram_tensor("dbgZT2", [D, S], F32, kind="ExternalOutput")
        d_dbgX1L = nc.dram_tensor("dbgX1L", [D, S], F32, kind="ExternalOutput")
        d_dbgY = [nc.dram_tensor(f"dbgY{k}", [D, S], F32, kind="ExternalOutput")
                  for k in range(4)]
        d_dbgQF = nc.dram_tensor("dbgQF", [128, 4 * 512], BF16, kind="ExternalOutput")
        d_dbgKF = nc.dram_tensor("dbgKF", [128, 4 * 512], BF16, kind="ExternalOutput")
        d_dbgVC = nc.dram_tensor("dbgVC", [128, 4 * 8 * 66], BF16, kind="ExternalOutput")
        d_dbgZF = nc.dram_tensor("dbgZF", [128, 4 * 512], BF16, kind="ExternalOutput")
        d_dbgPP = nc.dram_tensor("dbgPP", [128, 640], BF16, kind="ExternalOutput")
        d_dbgKI = nc.dram_tensor("dbgKI", [128, 6 * 512], F32, kind="ExternalOutput")
        d_dbgKP = nc.dram_tensor("dbgKP", [128, 4 * 512], BF16, kind="ExternalOutput")
        d_dbgTB = nc.dram_tensor("dbgTB", [128, 2048], F32, kind="ExternalOutput")
        d_dbgST = nc.dram_tensor("dbgST", [128, 8 * NT], F32, kind="ExternalOutput")
        d_dbgWQ = nc.dram_tensor("dbgWQ", [128, NC * D], BF16, kind="ExternalOutput")
    d_xT = nc.dram_tensor("sc_xT", [D, S], F32, kind=kd)
    d_zT = nc.dram_tensor("sc_zT", [D, S], F32, kind=kd)
    d_x1T = nc.dram_tensor("sc_x1T", [D, S], F32, kind=kd)
    d_yT = [nc.dram_tensor(f"sc_yT{i}", [D, S], F32, kind=kd) for i in range(2)]
    d_hT = [nc.dram_tensor(f"sc_hT{i}", [D, S], BF16, kind=kd) for i in range(2)]

    eye_b = nc.alloc_sbuf_tensor("eye_b", [128, 128], BF16)
    eye_f = nc.alloc_sbuf_tensor("eye_f", [128, 128], F32)
    ones_r = nc.alloc_sbuf_tensor("ones_r", [1, 128], F32)
    ones_cb = nc.alloc_sbuf_tensor("ones_cb", [128, 1], BF16)
    ones_cf = nc.alloc_sbuf_tensor("ones_cf", [128, 1], F32)
    mask_up = nc.alloc_sbuf_tensor("mask_up", [128, 128], F32)
    mask_lo = nc.alloc_sbuf_tensor("mask_lo", [128, 128], F32)
    mk_lt = nc.alloc_sbuf_tensor("mk_lt", [128, 128], BF16)
    mk_gt = nc.alloc_sbuf_tensor("mk_gt", [128, 128], BF16)
    negeye = nc.alloc_sbuf_tensor("negeye", [128, 128], BF16)
    cvals = nc.alloc_sbuf_tensor("cvals", [128, 4], F32)
    settle_s = nc.alloc_sbuf_tensor("settle_s", [128, 128], F32)
    settle_d = nc.alloc_sbuf_tensor("settle_d", [128, 128], F32)
    nc.const_aps.register(F32, 1e-5, cvals[:, 0:1])
    nc.const_aps.register(F32, LN8, cvals[:, 1:2])
    nc.const_aps.register(F32, 1e-30, cvals[:, 3:4])
    nc.const_aps.register(F32, 0.0, cvals[:, 2:3])

    stack = ExitStack()

    def _consts(g):
        g.memset(eye_b[:], 1.0)
        g.affine_select(eye_b[:], eye_b[:], pattern=[[1, 128]], compare_op=ALU.is_equal,
                        fill=0.0, base=0, channel_multiplier=-1)
        g.memset(eye_f[:], 1.0)
        g.affine_select(eye_f[:], eye_f[:], pattern=[[1, 128]], compare_op=ALU.is_equal,
                        fill=0.0, base=0, channel_multiplier=-1)
        g.memset(mask_up[:], 0.0)
        g.affine_select(mask_up[:], mask_up[:], pattern=[[1, 128]], compare_op=ALU.is_ge,
                        fill=NEGBIG, base=0, channel_multiplier=-1)
        g.memset(mask_lo[:], 0.0)
        g.affine_select(mask_lo[:], mask_lo[:], pattern=[[-1, 128]],
                        compare_op=ALU.is_ge, fill=NEGBIG, base=0,
                        channel_multiplier=1)
        g.memset(mk_lt[:], 1.0)
        g.affine_select(mk_lt[:], mk_lt[:], pattern=[[1, 128]], compare_op=ALU.is_ge,
                        fill=0.0, base=-1, channel_multiplier=-1)
        g.memset(mk_gt[:], 1.0)
        g.affine_select(mk_gt[:], mk_gt[:], pattern=[[-1, 128]], compare_op=ALU.is_ge,
                        fill=0.0, base=-1, channel_multiplier=1)
        g.memset(negeye[:], NEGBIG)
        return g.affine_select(negeye[:], negeye[:], pattern=[[1, 128]],
                               compare_op=ALU.is_equal, fill=0.0, base=0,
                               channel_multiplier=-1)

    p.op("gpsimd", _consts, inc="sL")

    def _vconsts(v):
        v.memset(ones_r[:], 1.0)
        v.memset(ones_cb[:], 1.0)
        v.memset(cvals[:, 0:1], 1e-5)
        v.memset(cvals[:, 1:2], LN8)
        v.memset(cvals[:, 2:3], 0.0)
        v.memset(cvals[:, 3:4], 1e-30)
        v.memset(settle_s[:], 0.0)
        return v.memset(ones_cf[:], 1.0)

    cv = p.op("vector", _vconsts, inc="sV")
    consts_v = (SEM["sV"], cv)

    stackA = ExitStack()
    wq_sb = stackA.enter_context(nc.sbuf_tensor("wq_sb", [128, NC * D], BF16))
    wk_sb = stackA.enter_context(nc.sbuf_tensor("wk_sb", [128, NC * D], BF16))
    wv_sb = stackA.enter_context(nc.sbuf_tensor("wv_sb", [128, NC * D], BF16))
    for dsrc, dst in ((d_wq, wq_sb), (d_wk, wk_sb), (d_wv, wv_sb)):
        p.op("gpsimd",
             lambda g, a=dst, b=dsrc: g.dma_start(
                 a[:].rearrange("q (c f) -> q c f", c=NC),
                 b[:].rearrange("(c q) f -> q c f", q=128)),
             inc="qWG", inck=16)
    w_outer = p.w("qWG")

    xn_fm = [stackA.enter_context(nc.sbuf_tensor(f"xn_fm{c}", [128, S], BF16))
             for c in range(NC)]

    stackA0 = ExitStack()
    a0_x = [stackA0.enter_context(nc.sbuf_tensor(f"a0_x{i}", [128, D], F32)) for i in range(2)]
    a0_xn = stackA0.enter_context(nc.sbuf_tensor("a0_xn", [128, D], F32))
    a0_xnb = [stackA0.enter_context(nc.sbuf_tensor(f"a0_xnb{i}", [128, D], BF16))
              for i in range(2)]
    a0_xTt = [stackA0.enter_context(nc.sbuf_tensor(f"a0_xTt{i}", [128, D], F32))
              for i in range(2)]
    a0_sq = stackA0.enter_context(nc.sbuf_tensor("a0_sq", [128, D], F32))
    a0_st = stackA0.enter_context(nc.sbuf_tensor("a0_st", [128, 8], F32))
    a0_ptb = [stackA0.enter_context(nc.psum_tensor(f"a0_ptb{i}", [128, 128], BF16))
              for i in range(2)]
    a0_ptf = [stackA0.enter_context(nc.psum_tensor(f"a0_ptf{i}", [128, 128], F32))
              for i in range(2)]

    tile_done = []
    vstats = []
    bevac = {}
    fevac = {}
    for i in range(NT):
        b = i % 2
        waits = [consts_v] if i == 0 else []
        if i >= 1:
            waits.append((SEM["sV"], vstats[i - 1]))
        if i >= 2:
            waits.append((SEM["sV"], tile_done[i - 2]))
        p.op("sync", lambda s, i=i, b=b: s.dma_start(
            a0_x[b][:], d_x[128 * i:128 * (i + 1), :]), waits=waits, inc="qL", inck=16)
        ldw = p.w("qL")

        p.op("vector", lambda v, b=b: v.reduce_sum(a0_st[:, 1:2], a0_x[b][:], AX), waits=[ldw])
        p.op("vector", lambda v, b=b: v.tensor_tensor(a0_sq[:], a0_x[b][:], a0_x[b][:], ALU.mult))
        vstat = p.op("vector", lambda v: v.reduce_sum(a0_st[:, 0:1], a0_sq[:], AX), inc="sV")
        vstats.append(vstat)

        p.op("scalar", lambda sc: sc.mul(a0_st[:, 2:3], a0_st[:, 1:2], -1.0 / D),
             waits=[(SEM["sV"], vstat)])
        p.op("scalar", lambda sc: sc.activation(a0_st[:, 3:4], a0_st[:, 1:2], AF.Square,
                                                scale=1.0 / D))
        p.op("scalar", lambda sc: sc.mul(a0_st[:, 3:4], a0_st[:, 3:4], -1.0))
        p.op("scalar", lambda sc: sc.mul(a0_st[:, 4:5], a0_st[:, 0:1], 1.0 / D))
        p.op("scalar", lambda sc: sc.add(a0_st[:, 4:5], a0_st[:, 4:5], a0_st[:, 3:4]))
        p.op("scalar", lambda sc: sc.activation(a0_st[:, 5:6], a0_st[:, 4:5], AF.Ln, bias=1e-5))
        crstd = p.op("scalar", lambda sc: sc.activation(a0_st[:, 5:6], a0_st[:, 5:6], AF.Exp,
                                                        scale=-0.5), inc="sC")

        p.op("vector", lambda v, b=b: v.tensor_scalar(
            a0_xn[:], a0_x[b][:], a0_st[:, 2:3], a0_st[:, 5:6], ALU.add, ALU.mult),
            waits=[(SEM["sC"], crstd)], inc="sV")
        vxnb = p.op("vector", lambda v, b=b: v.tensor_copy(a0_xnb[b][:], a0_xn[:]), inc="sV")
        if debug:
            p.op("sync", lambda s_, i=i: s_.dma_start(d_dbgXNT[128 * i:128 * (i + 1), :],
                                                      a0_xn[:]),
                 waits=[(SEM["sV"], vxnb)], inc="qD", inck=16)
            p.op("sync", lambda s_, i=i: s_.dma_start(d_dbgST[:, 8 * i:8 * (i + 1)],
                                                      a0_st[:]),
                 inc="qD", inck=16)

        for c in range(NC):
            tw = [(SEM["sV"], vxnb)] if c == 0 else []
            if i == 0 and c == 0:
                tw.append((SEM["sL"], 1))
            if (i, c - 2) in bevac:
                tw.append((SEM["sC"], bevac[(i, c - 2)]))
            elif (i - 1, c + 6) in bevac:
                tw.append((SEM["sC"], bevac[(i - 1, c + 6)]))
            tp = p.op("tensor", lambda t, b=b, c=c: t.transpose(
                a0_ptb[c % 2][:], a0_xnb[b][:, 128 * c:128 * (c + 1)], eye_b[:]),
                waits=tw, inc="sP")
            bevac[(i, c)] = p.op("scalar", lambda sc, i=i, c=c: sc.copy(
                xn_fm[c][:, 128 * i:128 * (i + 1)], a0_ptb[c % 2][:]),
                waits=[(SEM["sP"], tp)], inc="sC")
        for c in range(NC):
            tw = []
            if (i, c - 2) in fevac:
                tw.append((SEM["sV"], fevac[(i, c - 2)]))
            elif (i - 1, c + 6) in fevac:
                tw.append((SEM["sV"], fevac[(i - 1, c + 6)]))
            tp = p.op("tensor", lambda t, b=b, c=c: t.transpose(
                a0_ptf[c % 2][:], a0_x[b][:, 128 * c:128 * (c + 1)], eye_f[:]),
                waits=tw, inc="sP")
            ew = [(SEM["sP"], tp)]
            if c == 0 and i >= 2:
                ew.append(p.w("qS1"))
            fevac[(i, c)] = p.op("vector", lambda v, b=b, c=c: v.tensor_copy(
                a0_xTt[b][:, 128 * c:128 * (c + 1)], a0_ptf[c % 2][:]),
                waits=ew, inc="sV")
        tile_done.append(p.cnt["sV"])

        p.op("sync", lambda s, i=i, b=b: s.dma_start(
            d_xT[:].rearrange("(c q) s_ -> q c s_", q=128)[:, :, 128 * i:128 * (i + 1)],
            a0_xTt[b][:].rearrange("q (c f) -> q c f", c=NC)),
            waits=[(SEM["sV"], tile_done[i])], inc="qS1", inck=16)

    a0_done = p.snap()
    stackA0.close()

    Qf = stackA.enter_context(nc.sbuf_tensor("a1_Q", [128, S], BF16))
    Kf = stackA.enter_context(nc.sbuf_tensor("a1_K", [128, S], BF16))
    Vfm = stackA.enter_context(nc.sbuf_tensor("a1_Vfm", [128, S], F32))
    Vaug = stackA.enter_context(nc.sbuf_tensor("a1_Vaug", [128, NT * 132], BF16))
    Pt = [stackA.enter_context(nc.sbuf_tensor(f"a1_P{i}", [128, 1024], BF16)) for i in range(2)]
    a1_negm = stackA.enter_context(nc.sbuf_tensor("a1_negm", [128, 2], F32))
    a1_ms = stackA.enter_context(nc.sbuf_tensor("a1_ms", [128, 4], F32))
    a1_mr = stackA.enter_context(nc.sbuf_tensor("a1_mr", [1, 256], F32))
    a1_mt = stackA.enter_context(nc.sbuf_tensor("a1_mt", [1, 8], F32))
    a1_rl = stackA.enter_context(nc.sbuf_tensor("a1_rl", [1, 1024], F32))
    a1_zs = stackA.enter_context(nc.sbuf_tensor("a1_zs", [64, 1024], F32))
    a1_zn = stackA.enter_context(nc.sbuf_tensor("a1_zn", [64, 1024], F32))

    STd = [stackA.enter_context(nc.psum_tensor(f"a1_ST{i}", [128, 1024], F32))
           for i in range(2)]
    ZP = [stackA.enter_context(nc.psum_tensor(f"a1_Z{h}", [65, 1024], F32)) for h in range(2)]

    barA1 = p.barrier()

    p.op("vector", lambda v: v.memset(
        Vaug[:].rearrange("q (c f) -> q c f", c=NT)[:, :, 64:65], 1.0),
        waits=barA1("vector"))
    p.op("vector", lambda v: v.memset(
        Vaug[:].rearrange("q (c f) -> q c f", c=NT)[:, :, 129:130], 1.0))

    genfree = 0
    for g_ in range(8):
        qk_ev = {}
        for which, wsb, dstf in ((0, wq_sb, Qf), (1, wk_sb, Kf)):
            for nt in range(8):
                pb = STd[nt % 2][:, 0:512]
                idx = which * 8 + nt
                tw = []
                if g_ == 0 and idx == 0:
                    tw = barA1("tensor", [w_outer, (SEM["sC"], a0_done["sC"]),
                                          (SEM["sV"], a0_done["sV"])])
                if idx >= 2:
                    tw.append((SEM["sV"], qk_ev[idx - 2]))

                def mmqk(t, wsb=wsb, g_=g_, nt=nt, pb=pb):
                    for c in range(NC):
                        ins = t.matmul(pb,
                                       wsb[:, c * D + 128 * g_:c * D + 128 * (g_ + 1)],
                                       xn_fm[c][:, 512 * nt:512 * (nt + 1)],
                                       start=(c == 0), stop=(c == NC - 1))
                    return ins
                mq = p.op("tensor", mmqk, waits=tw, inc="sP")
                if which == 0:
                    fn = (lambda v, nt=nt, pb=pb, dd=dstf:
                          v.tensor_scalar_mul(dd[:, 512 * nt:512 * (nt + 1)], pb, 0.125))
                else:
                    fn = (lambda v, nt=nt, pb=pb, dd=dstf:
                          v.tensor_copy(dd[:, 512 * nt:512 * (nt + 1)], pb))
                ew = barA1("vector") if (g_ == 0 and idx == 0) else []
                qk_ev[idx] = p.op("vector", fn, waits=ew + [(SEM["sP"], mq)], inc="sV")
        v_ev = {}
        for nt in range(8):
            pb = STd[nt % 2][:, 0:512]
            tw = [(SEM["sV"], qk_ev[14 + nt % 2])] if nt < 2 else [(SEM["sV"], v_ev[nt - 2])]

            def mmv(t, g_=g_, nt=nt, pb=pb):
                for c in range(NC):
                    ins = t.matmul(pb,
                                   wv_sb[:, c * D + 128 * g_:c * D + 128 * (g_ + 1)],
                                   xn_fm[c][:, 512 * nt:512 * (nt + 1)],
                                   start=(c == 0), stop=(c == NC - 1))
                return ins
            mv = p.op("tensor", mmv, waits=tw, inc="sP")
            v_ev[nt] = p.op("vector", lambda v, nt=nt, pb=pb: v.tensor_copy(
                Vfm[:, 512 * nt:512 * (nt + 1)], pb),
                waits=[(SEM["sP"], mv)], inc="sV")
        va_ev = {}
        for tt in range(NT):
            pb = STd[tt % 2][:, 512:640]
            tw = [(SEM["sV"], v_ev[6 + tt % 2])] if tt < 2 else [(SEM["sV"], va_ev[tt - 2])]
            tp = p.op("tensor", lambda t, tt=tt, pb=pb: t.transpose(
                pb, Vfm[:, 128 * tt:128 * (tt + 1)], eye_f[:]),
                waits=tw, inc="sP")
            va_ev[tt] = p.op("vector", lambda v, tt=tt, pb=pb: v.tensor_copy(
                Vaug[:, 132 * tt:132 * tt + 130]
                    .rearrange("q (s f) -> q s f", f=65)[:, :, 0:64],
                pb.rearrange("q (s f) -> q s f", s=2)),
                waits=[(SEM["sP"], tp)], inc="sV")

        p.op("vector", lambda v: v.tensor_reduce(a1_ms[:, 0:1], Qf[:], AX, ALU.max,
                                                 apply_absolute_value=True))
        vk = p.op("vector", lambda v: v.tensor_reduce(a1_ms[:, 1:2], Kf[:], AX, ALU.max,
                                                      apply_absolute_value=True), inc="sV")
        p.op("tensor", lambda t: t.transpose(STd[0][0:1, 0:128], a1_ms[:, 0:1], eye_f[:]),
             waits=[(SEM["sV"], vk)])
        tmr = p.op("tensor", lambda t: t.transpose(STd[0][0:1, 128:256], a1_ms[:, 1:2],
                                                   eye_f[:]), inc="sP")
        p.op("vector", lambda v: v.tensor_copy(a1_mr[0:1, :], STd[0][0:1, 0:256]),
             waits=[(SEM["sP"], tmr)])
        p.op("vector", lambda v: v.tensor_copy(settle_d[:], settle_s[:]))
        p.op("vector", lambda v: v.reduce_max(a1_mt[0:1, 0:1], a1_mr[0:1, 0:128], AX))
        p.op("vector", lambda v: v.reduce_max(a1_mt[0:1, 1:2], a1_mr[0:1, 128:256], AX))
        p.op("vector", lambda v: v.tensor_copy(settle_d[:], settle_s[:]))
        p.op("vector", lambda v: v.tensor_tensor(a1_mt[0:1, 4:5], a1_mt[0:1, 0:1],
                                                 a1_mt[0:1, 1:2], ALU.mult))
        p.op("vector", lambda v: v.tensor_copy(settle_d[:], settle_s[:]))
        vmt = p.op("vector", lambda v: v.tensor_scalar_mul(a1_mt[0:1, 6:7], a1_mt[0:1, 4:5],
                                                           -64.0), inc="sV")
        tbc = p.op("tensor", lambda t: t.matmul(STd[0][0:128, 128:129], ones_r[0:1, :],
                                                a1_mt[0:1, 6:7], start=True, stop=True),
                   waits=[(SEM["sV"], vmt)], inc="sP")
        vnm = p.op("vector", lambda v: v.tensor_copy(a1_negm[:, 0:1], STd[0][0:128, 128:129]),
                   waits=[(SEM["sP"], tbc)], inc="sV")

        if debug and g_ == 0:
            for c in range(NC):
                p.op("sync", lambda s_, c=c: s_.dma_start(
                    d_dbgXN[:, S * c:S * (c + 1)], xn_fm[c][:]),
                    waits=[(SEM["sV"], vnm)], inc="qD", inck=16)
            p.op("sync", lambda s_: s_.dma_start(d_dbgWQ[:], wq_sb[:]), inc="qD", inck=16)
            p.op("sync", lambda s_: s_.dma_start(d_dbgQ[:], Qf[:]),
                 waits=barA1("sync") + [(SEM["sV"], vnm)], inc="qD", inck=16)
            p.op("sync", lambda s_: s_.dma_start(d_dbgK[:], Kf[:]), inc="qD", inck=16)
            p.op("sync", lambda s_: s_.dma_start(d_dbgV[:], Vaug[:]), inc="qD", inck=16)
            p.op("sync", lambda s_: s_.dma_start(d_dbgM[:], a1_negm[:]),
                 inc="qD", inck=16)
        for qh in range(8):
            zh = qh % 2
            expc = {}
            stc = {}
            pvc = {}
            for c in range(34):
                if c < 32:
                    par = c % 2
                    tw = []
                    if qh == 0 and c == 0:
                        tw.append((SEM["sV"], vnm))
                    if c >= 2:
                        tw.append((SEM["sC"], expc[c - 2]))

                    def mmst(t, qh=qh, c=c, par=par):
                        for h in range(2):
                            ins = t.matmul(
                                STd[par][:, 512 * h:512 * (h + 1)],
                                Kf[64 * h:64 * (h + 1), 128 * c:128 * (c + 1)],
                                Qf[64 * h:64 * (h + 1), 512 * qh:512 * (qh + 1)],
                                start=True, stop=True)
                        return ins
                    stc[c] = p.op("tensor", mmst, waits=tw, inc="sP")
                if c >= 2:
                    cc = c - 2

                    def mmpv(t, cc=cc, zh=zh):
                        for h in range(2):
                            ins = t.matmul(
                                ZP[h][0:65, 512 * zh:512 * (zh + 1)],
                                Vaug[:, 132 * cc + 65 * h:132 * cc + 65 * h + 65],
                                Pt[cc % 2][:, 512 * h:512 * (h + 1)],
                                start=(cc == 0), stop=(cc == 31))
                        return ins
                    pvc[cc] = p.op("tensor", mmpv, waits=[(SEM["sC"], expc[cc])], inc="sP")
                if c < 32:
                    ew = barA1("scalar") if (g_ == 0 and qh == 0 and c == 0) else []
                    pw = (SEM["sP"], pvc[c - 2]) if c >= 2 else (SEM["sP"], stc[c])
                    expc[c] = p.op("scalar", lambda sc, c=c: sc.activation(
                        Pt[c % 2][:], STd[c % 2][:], AF.Exp, bias=a1_negm[:, 0:1]),
                        waits=ew + [pw, (SEM["sV"], vnm)], inc="sC")

            if zh == 1:
                qb = qh // 2
                zdone = []
                trb_prev = None
                for h in range(2):
                    rw = [(SEM["sP"], pvc[31])]
                    if trb_prev is not None:
                        rw.append((SEM["sP"], trb_prev))
                    p.op("vector", lambda v, h=h: v.reciprocal(a1_rl[:], ZP[h][64:65, :]),
                         waits=rw)
                    vz = p.op("vector", lambda v, h=h: v.tensor_copy(a1_zs[:], ZP[h][0:64, :]),
                              inc="sV")

                    def mrb(t, h=h):
                        for n in range(2):
                            ins = t.matmul(STd[n][0:64, 512 * h:512 * (h + 1)],
                                           ones_r[0:1, 0:64],
                                           a1_rl[0:1, 512 * n:512 * (n + 1)],
                                           start=True, stop=True)
                        return ins
                    trb = p.op("tensor", mrb, waits=[(SEM["sV"], vz)], inc="sP")
                    trb_prev = trb
                    zw = [(SEM["sP"], trb), p.w("qZ")]

                    def vznf(v, h=h):
                        for n in range(2):
                            ins = v.tensor_tensor(
                                a1_zn[:, 512 * n:512 * (n + 1)], a1_zs[:, 512 * n:512 * (n + 1)],
                                STd[n][0:64, 512 * h:512 * (h + 1)], ALU.mult)
                        return ins
                    vzn = p.op("vector", vznf, waits=zw, inc="sV")
                    st = p.op("sync", lambda s, g_=g_, h=h, qb=qb: s.dma_start(
                        d_zT[128 * g_ + 64 * h:128 * g_ + 64 * (h + 1),
                             1024 * qb:1024 * (qb + 1)], a1_zn[:]),
                        waits=barA1("sync") + [(SEM["sV"], vzn)], inc="qZ", inck=16)
                    zdone.append((vzn, st))
                guard = p.op("tensor", lambda t: t.matmul(
                    STd[0][0:1, 0:1], ones_r[0:1, 0:1], ones_r[0:1, 0:1],
                    start=True, stop=True),
                    waits=[(SEM["sV"], zdone[1][0])], inc="sP")

    a1_done = p.snap()
    stackA.close()

    stackB0 = ExitStack()
    b_xc = [stackB0.enter_context(nc.sbuf_tensor(f"b0_x{c}", [128, 512], F32))
            for c in range(NC)]
    b_zc = [stackB0.enter_context(nc.sbuf_tensor(f"b0_z{c}", [128, 512], F32))
            for c in range(NC)]
    b_x1 = [stackB0.enter_context(nc.sbuf_tensor(f"b0_x1{c}", [128, 512], F32))
            for c in range(NC)]
    b_sq = [stackB0.enter_context(nc.sbuf_tensor(f"b0_sq{i}", [128, 512], F32))
            for i in range(2)]
    b_y0 = [stackB0.enter_context(nc.sbuf_tensor(f"b0_y0{i}", [128, 512], F32))
            for i in range(2)]
    b_t = stackB0.enter_context(nc.sbuf_tensor("b0_t", [128, 512], F32))
    b_stats = stackB0.enter_context(nc.sbuf_tensor("b0_stats", [1, 4 * 512], F32))
    b_S1 = stackB0.enter_context(nc.psum_tensor("b0_S1", [1, 512], F32))
    b_S2 = stackB0.enter_context(nc.psum_tensor("b0_S2", [1, 512], F32))
    b_MB = stackB0.enter_context(nc.psum_tensor("b0_MB", [128, 512], F32))
    b_RB = stackB0.enter_context(nc.psum_tensor("b0_RB", [128, 512], F32))

    barB0 = p.barrier()
    if debug:
        p.op("sync", lambda s_: s_.dma_start(d_dbgZT2[:], d_zT[:]),
             waits=barB0("sync"), inc="qD", inck=16)
    prev_grp = None
    for G in range(8):
        for c in range(NC):
            tw = barB0("sync") if (G == 0 and c == 0) else []
            if prev_grp is not None and c == 0:
                tw.append((SEM["sV"], prev_grp))
            p.op("sync", lambda s, c=c, G=G: s.dma_start(
                b_xc[c][:], d_xT[128 * c:128 * (c + 1), 512 * G:512 * (G + 1)]),
                waits=tw, inc="qL", inck=16)
            p.op("sync", lambda s, c=c, G=G: s.dma_start(
                b_zc[c][:], d_zT[128 * c:128 * (c + 1), 512 * G:512 * (G + 1)]),
                inc="qL", inck=16)
        ld_wave = p.w("qL")
        adds = []
        for c in range(NC):
            tw = [ld_wave, p.w("qS2")]
            if G == 0 and c == 0:
                tw = barB0("vector") + tw
            adds.append(p.op("vector", lambda v, c=c: v.tensor_tensor(
                b_x1[c][:], b_xc[c][:], b_zc[c][:], ALU.add), waits=tw, inc="sV"))
            p.op("sync", lambda s, c=c, G=G: s.dma_start(
                d_x1T[128 * c:128 * (c + 1), 512 * G:512 * (G + 1)], b_x1[c][:]),
                waits=[(SEM["sV"], adds[-1])], inc="qS2", inck=16)
        m1l = None
        for c in range(NC):
            tw = barB0("tensor") if (G == 0 and c == 0) else []
            m1l = p.op("tensor", lambda t, c=c: t.matmul(
                b_S1[:], ones_cf[:], b_x1[c][:], start=(c == 0), stop=(c == NC - 1)),
                waits=tw + [(SEM["sV"], adds[c])], inc="sP")
        sqd = []
        for c in range(NC):
            tw = [] if c < 2 else [(SEM["sP"], sqd[c - 2])]
            vsq = p.op("vector", lambda v, c=c: v.tensor_tensor(
                b_sq[c % 2][:], b_x1[c][:], b_x1[c][:], ALU.mult), waits=tw, inc="sV")
            sqd.append(p.op("tensor", lambda t, c=c: t.matmul(
                b_S2[:], ones_cf[:], b_sq[c % 2][:], start=(c == 0), stop=(c == NC - 1)),
                waits=[(SEM["sV"], vsq)], inc="sP"))
        p.op("vector", lambda v: v.tensor_scalar_mul(b_stats[0:1, 0:512], b_S1[:],
                                                     -1.0 / D), waits=[(SEM["sP"], sqd[7])])
        p.op("vector", lambda v: v.tensor_scalar_mul(b_stats[0:1, 512:1024], b_S2[:], 1.0 / D))
        p.op("vector", lambda v: v.tensor_copy(settle_d[:], settle_s[:]))
        p.op("vector", lambda v: v.tensor_tensor(b_stats[0:1, 1024:1536], b_stats[0:1, 0:512],
                                                 b_stats[0:1, 0:512], ALU.mult))
        p.op("vector", lambda v: v.tensor_copy(settle_d[:], settle_s[:]))
        vvar = p.op("vector", lambda v: v.tensor_tensor(
            b_stats[0:1, 1024:1536], b_stats[0:1, 512:1024], b_stats[0:1, 1024:1536],
            ALU.subtract), inc="sV")
        tw = barB0("scalar") if G == 0 else []
        p.op("scalar", lambda sc: sc.activation(b_stats[0:1, 1536:2048],
                                                b_stats[0:1, 1024:1536], AF.Ln, bias=1e-5),
             waits=tw + [(SEM["sV"], vvar)])
        crs = p.op("scalar", lambda sc: sc.activation(b_stats[0:1, 1536:2048],
                                                      b_stats[0:1, 1536:2048], AF.Exp,
                                                      scale=-0.5), inc="sC")
        mmb = p.op("tensor", lambda t: t.matmul(b_MB[:], ones_r[:], b_stats[0:1, 0:512],
                                                start=True, stop=True),
                   waits=[(SEM["sC"], crs)], inc="sP")
        mrb2 = p.op("tensor", lambda t: t.matmul(b_RB[:], ones_r[:], b_stats[0:1, 1536:2048],
                                                 start=True, stop=True), inc="sP")
        if debug:
            for c in range(NC):
                p.op("sync", lambda s_, c=c, G=G: s_.dma_start(
                    d_dbgX1L[128 * c:128 * (c + 1), 512 * G:512 * (G + 1)], b_x1[c][:]),
                    waits=[(SEM["sP"], mrb2)], inc="qD", inck=16)
        vy = None
        for c in range(NC):
            nw = [(SEM["sP"], mrb2)]
            if c < 2 or True:
                nw.append(p.w("qS3"))
                nw.append(p.w("qHG"))
            p.op("vector", lambda v, c=c: v.tensor_tensor(
                b_t[:], b_x1[c][:], b_MB[:], ALU.add), waits=nw, inc="sV")
            vy = p.op("vector", lambda v, c=c: v.tensor_tensor(
                b_y0[c % 2][:], b_t[:], b_RB[:], ALU.mult), inc="sV")
            p.op("sync", lambda s, c=c, G=G: s.dma_start(
                d_yT[0][128 * c:128 * (c + 1), 512 * G:512 * (G + 1)], b_y0[c % 2][:]),
                waits=[(SEM["sV"], vy)], inc="qS3", inck=16)
            tw = barB0("gpsimd") if (G == 0 and c == 0) else []
            p.op("gpsimd", lambda g, c=c, G=G: g.dma_start(
                d_hT[0][128 * c:128 * (c + 1), 512 * G:512 * (G + 1)], b_y0[c % 2][:]),
                waits=tw + [(SEM["sV"], vy)], inc="qHG", inck=16)
        prev_grp = vy

    b0_done = p.snap()
    stackB0.close()

    def _build_layer(l):
        yin, hin, yout, hout = d_yT[0], d_hT[0], d_yT[1], d_hT[1]

        stackIA = ExitStack()
        iwq = stackIA.enter_context(nc.sbuf_tensor(f"l{l}_ia_wqkv", [128, NC * 1536], BF16))
        iwo = stackIA.enter_context(nc.sbuf_tensor(f"l{l}_ia_wo", [128, 4 * D], BF16))
        tb = [stackIA.enter_context(nc.sbuf_tensor(f"l{l}_ia_tb{k}", [128, 2048], F32))
              for k in range(4)]
        qsb = stackIA.enter_context(nc.sbuf_tensor(f"l{l}_ia_qsb", [128, 64], F32))
        ksb = stackIA.enter_context(nc.sbuf_tensor(f"l{l}_ia_ksb", [128, 64], F32))
        qsw = stackIA.enter_context(nc.sbuf_tensor(f"l{l}_ia_qsw", [128, 64], F32))
        ksw = stackIA.enter_context(nc.sbuf_tensor(f"l{l}_ia_ksw", [128, 64], F32))
        qrow = stackIA.enter_context(nc.sbuf_tensor(f"l{l}_ia_qrow", [1, 64], F32))
        krow = stackIA.enter_context(nc.sbuf_tensor(f"l{l}_ia_krow", [1, 64], F32))
        ht = [stackIA.enter_context(nc.sbuf_tensor(f"l{l}_ia_h{c}", [128, 512], BF16))
              for c in range(NC)]
        yt = [stackIA.enter_context(nc.sbuf_tensor(f"l{l}_ia_y{c}", [128, 512], F32))
              for c in range(NC)]
        qev = stackIA.enter_context(nc.sbuf_tensor(f"l{l}_ia_qev", [128, 512], F32))
        sqs = stackIA.enter_context(nc.sbuf_tensor(f"l{l}_ia_sqs", [128, 512], F32))
        ssum = stackIA.enter_context(nc.sbuf_tensor(f"l{l}_ia_ssum", [128, 16], F32))
        rot = stackIA.enter_context(nc.sbuf_tensor(f"l{l}_ia_rot", [128, 512], F32))
        t1s = stackIA.enter_context(nc.sbuf_tensor(f"l{l}_ia_t1", [128, 512], F32))
        qpb8 = [stackIA.enter_context(nc.sbuf_tensor(f"l{l}_ia_qpb{i}", [128, 512], BF16))
                for i in range(8)]
        ia_state = {"crot": None, "vqp": None, "ppar": 0, "evq": []}
        tp_last = {}
        qfm = stackIA.enter_context(nc.sbuf_tensor(f"l{l}_ia_qfm", [128, 4 * 512], BF16))
        kfmA = stackIA.enter_context(nc.sbuf_tensor(f"l{l}_ia_kfmA", [128, 4 * 512], BF16))
        kfmB = stackIA.enter_context(nc.sbuf_tensor(f"l{l}_ia_kfmB", [128, 4 * 512], BF16))
        VA = stackIA.enter_context(nc.sbuf_tensor(f"l{l}_ia_VA", [128, 4 * 8 * 66], BF16))
        VB = stackIA.enter_context(nc.sbuf_tensor(f"l{l}_ia_VB", [128, 4 * 8 * 66], BF16))
        Pe = stackIA.enter_context(nc.sbuf_tensor(f"l{l}_ia_Pe", [128, 640], BF16))
        Po = stackIA.enter_context(nc.sbuf_tensor(f"l{l}_ia_Po", [128, 640], BF16))
        PTe = stackIA.enter_context(nc.sbuf_tensor(f"l{l}_ia_PTe", [128, 640], BF16))
        PTo = stackIA.enter_context(nc.sbuf_tensor(f"l{l}_ia_PTo", [128, 640], BF16))
        nme = stackIA.enter_context(nc.sbuf_tensor(f"l{l}_ia_nme", [128, 1], F32))
        nmo = stackIA.enter_context(nc.sbuf_tensor(f"l{l}_ia_nmo", [128, 1], F32))
        irl = stackIA.enter_context(nc.sbuf_tensor(f"l{l}_ia_irl", [1, 512], F32))
        izs = stackIA.enter_context(nc.sbuf_tensor(f"l{l}_ia_izs", [64, 512], F32))
        zfm = stackIA.enter_context(nc.sbuf_tensor(f"l{l}_ia_zfm", [128, 4 * 512], BF16))
        y1e = [stackIA.enter_context(nc.sbuf_tensor(f"l{l}_ia_y1e{i}", [128, 512], F32))
               for i in range(2)]

        PSA = stackIA.enter_context(nc.psum_tensor(f"l{l}_ia_PSA", [128, 1536], F32))
        PSB = stackIA.enter_context(nc.psum_tensor(f"l{l}_ia_PSB", [128, 1536], F32))
        PTAp = stackIA.enter_context(nc.psum_tensor(f"l{l}_ia_PTAp", [128, 640], BF16))
        PTBp = stackIA.enter_context(nc.psum_tensor(f"l{l}_ia_PTBp", [128, 640], BF16))

        barIA = p.barrier()

        p.op("gpsimd", lambda g, l=l: g.dma_start(
            iwq[:].rearrange("q (c f) -> q c f", c=NC),
            d_iwqkv[l][:].rearrange("(c q) f -> q c f", q=128)),
            waits=barIA("gpsimd"), inc="qWG", inck=16)
        p.op("gpsimd", lambda g, l=l: g.dma_start(
            iwo[:].rearrange("q (c f) -> q c f", c=4),
            d_iwo[l][:].rearrange("(c q) f -> q c f", q=128)), inc="qWG", inck=16)
        wld = p.w("qWG")
        for k in range(4):
            p.op("sync", lambda s, k=k: s.dma_start(
                tb[k][:].rearrange("q (c f) -> q c f", f=64),
                d_tab[k][:].rearrange("(c q) f -> q c f", q=128)),
                waits=barIA("sync"), inc="qL", inck=16)
        p.op("sync", lambda s, l=l: s.dma_start(qrow[:], d_qs[l][:]), inc="qL", inck=16)
        p.op("sync", lambda s, l=l: s.dma_start(krow[:], d_ks[l][:]), inc="qL", inck=16)
        tabld = p.w("qL")
        tq = p.op("tensor", lambda t: t.matmul(PSA[0:128, 0:64], ones_r[:], qrow[:],
                                               start=True, stop=True),
                  waits=barIA("tensor") + [tabld], inc="sP")
        tk = p.op("tensor", lambda t: t.matmul(PSB[0:128, 0:64], ones_r[:], krow[:],
                                               start=True, stop=True), inc="sP")
        vq1 = p.op("vector", lambda v: v.tensor_copy(qsb[:], PSA[0:128, 0:64]),
                   waits=barIA("vector") + [(SEM["sP"], tq)], inc="sV")
        vk1 = p.op("vector", lambda v: v.tensor_copy(ksb[:], PSB[0:128, 0:64]),
                   waits=[(SEM["sP"], tk)], inc="sV")
        p.op("vector", lambda v: v.tensor_copy(qsw[:, 0:32], qsb[:, 32:64]))
        p.op("vector", lambda v: v.tensor_copy(qsw[:, 32:64], qsb[:, 0:32]))
        p.op("vector", lambda v: v.tensor_copy(ksw[:, 0:32], ksb[:, 32:64]))
        p.op("vector", lambda v: v.tensor_copy(ksw[:, 32:64], ksb[:, 0:32]))
        for k, scl in ((0, qsb), (1, qsw), (2, ksb), (3, ksw)):
            p.op("vector", lambda v, k=k, scl=scl: v.tensor_tensor(
                tb[k][:].rearrange("q (c f) -> q c f", f=64),
                tb[k][:].rearrange("q (c f) -> q c f", f=64),
                scl[:].unsqueeze(1).broadcast_to([128, 32, 64]), ALU.mult),
                waits=[tabld])
        vfold = p.op("vector", lambda v: v.tensor_copy(qsw[0:1, 0:1], qsw[0:1, 0:1]), inc="sV")

        kprev, kcur = kfmA, kfmB
        Vprev, Vcur = VA, VB
        for vb_ in (VA, VB):
            p.op("vector", lambda v, vb_=vb_: v.memset(
                vb_[:].rearrange("q (c f) -> q c f", f=66)[:, :, 64:65], 1.0))

        ia_done_prev_w = None
        ia_z = {"last_exp": None, "zg": [None, None]}
        for w in range(8):
            kprev, kcur = kcur, kprev
            Vprev, Vcur = Vcur, Vprev
            hmemv = 0
            for c in range(NC):
                tw = []
                if ia_done_prev_w is not None and c == 0:
                    tw = [(SEM["sP"], ia_done_prev_w[0]), (SEM["sV"], ia_done_prev_w[1])]
                if c < 4 or w > 0:
                    off = 512 * w if c < 4 else 512 * w - 1
                    p.op("sync", lambda s, c=c, off=off: s.dma_start(
                        ht[c][:], hin[128 * c:128 * (c + 1), off:off + 512]),
                        waits=tw, inc="qL", inck=16)
                else:
                    hmemv = p.op("vector", lambda v, c=c: v.memset(ht[c][:, 0:1], 0.0),
                                 waits=tw, inc="sV")
                    p.op("sync", lambda s, c=c: s.dma_start(
                        ht[c][:, 1:512], hin[128 * c:128 * (c + 1), 0:511]),
                        waits=tw, inc="qL", inck=16)
            for c in range(NC):
                p.op("sync", lambda s, c=c, w=w: s.dma_start(
                    yt[c][:], yin[128 * c:128 * (c + 1), 512 * w:512 * (w + 1)]),
                    inc="qL", inck=16)
            hldw = p.w("qL")

            qk_psum_free = []
            pmarks = {}

            def emit_gen(sub):
                PQ = PSA if sub % 2 == 0 else PSB
                tw = [hldw, (SEM["sV"], vfold)]
                if w == 0 and sub == 0:
                    tw.append(wld)
                    tw.append((SEM["sV"], hmemv))
                if len(qk_psum_free) >= 2:
                    tw.append((SEM["sV"], qk_psum_free[-2]))

                def mmqkv(t, sub=sub, PQ=PQ):
                    for c in range(NC):
                        for j in range(3):
                            ins = t.matmul(PQ[:, 512 * j:512 * (j + 1)],
                                           ht[c][:, 128 * sub:128 * (sub + 1)],
                                           iwq[:, 1536 * c + 512 * j:1536 * c + 512 * (j + 1)],
                                           start=(c == 0), stop=(c == NC - 1))
                    return ins
                return p.op("tensor", mmqkv, waits=tw, inc="sP"), PQ

            def emit_proc(sub, mql, PQ):
                st_ = ia_state
                p.op("vector", lambda v, sub=sub, PQ=PQ, Vcur=Vcur: v.tensor_copy(
                    Vcur[:].rearrange("q (cc hh f) -> q cc hh f", cc=4, hh=8)[:, sub, :, 0:64],
                    PQ[:, 1024:1536].rearrange("q (hh f) -> q hh f", hh=8)),
                    waits=[(SEM["sP"], mql)])
                pmarks[sub] = []
                vev = None
                for iq, (dst, t_cs, t_ss, bias) in enumerate(
                        ((qfm, tb[0], tb[1], LN8), (kcur, tb[2], tb[3], 0.0))):
                    off = 512 * iq
                    buf = qpb8[2 * sub + iq]
                    ew = [(SEM["sP"], mql)]
                    if st_["crot"] is not None:
                        ew.append((SEM["sC"], st_["crot"]))
                    vev = p.op("vector", lambda v, PQ=PQ, off=off: v.tensor_copy(
                        qev[:], PQ[:, off:off + 512]), waits=ew, inc="sV")
                    cw = barIA("scalar") if (w == 0 and sub == 0 and iq == 0) else []
                    csq = p.op("scalar", lambda sc: sc.activation(sqs[:], qev[:], AF.Square),
                               waits=cw + [(SEM["sV"], vev)], inc="sC")
                    vs2 = p.op("vector", lambda v: v.tensor_reduce(
                        ssum[:, 0:8], sqs[:].rearrange("q (hh f) -> q hh f", hh=8), AX,
                        ALU.add), waits=[(SEM["sC"], csq)], inc="sV")
                    p.op("scalar", lambda sc: sc.activation(ssum[:, 8:16], ssum[:, 0:8],
                                                            AF.Ln, bias=1e-30),
                         waits=[(SEM["sV"], vs2)])
                    crs2 = p.op("scalar", lambda sc, bias=bias: sc.activation(
                        ssum[:, 0:8], ssum[:, 8:16], AF.Exp, scale=-0.5, bias=bias),
                        inc="sC")
                    rw = [(SEM["sC"], crs2)]
                    if st_["vqp"] is not None:
                        rw.append((SEM["sV"], st_["vqp"]))
                    p.op("scalar", lambda sc: sc.mul(
                        rot[:].rearrange("q (hh f) -> q hh f", hh=8)[:, :, 0:32],
                        qev[:].rearrange("q (hh f) -> q hh f", hh=8)[:, :, 32:64], -1.0),
                        waits=rw)
                    st_["crot"] = crot = p.op("scalar", lambda sc: sc.copy(
                        rot[:].rearrange("q (hh f) -> q hh f", hh=8)[:, :, 32:64],
                        qev[:].rearrange("q (hh f) -> q hh f", hh=8)[:, :, 0:32]), inc="sC")
                    t1w = []
                    if st_["vqp"] is not None:
                        t1w.append((SEM["sV"], st_["vqp"]))
                    p.op("vector", lambda v, t_cs=t_cs, w=w, sub=sub: v.tensor_tensor(
                        t1s[:].rearrange("q (hh f) -> q hh f", hh=8),
                        qev[:].rearrange("q (hh f) -> q hh f", hh=8),
                        t_cs[:, 64 * (4 * w + sub):64 * (4 * w + sub) + 64]
                            .unsqueeze(1).broadcast_to([128, 8, 64]), ALU.mult), waits=t1w)
                    p.op("vector", lambda v, t_ss=t_ss, w=w, sub=sub: v.tensor_tensor(
                        rot[:].rearrange("q (hh f) -> q hh f", hh=8),
                        rot[:].rearrange("q (hh f) -> q hh f", hh=8),
                        t_ss[:, 64 * (4 * w + sub):64 * (4 * w + sub) + 64]
                            .unsqueeze(1).broadcast_to([128, 8, 64]), ALU.mult),
                        waits=[(SEM["sC"], crot)])
                    p.op("vector", lambda v: v.tensor_tensor(t1s[:], t1s[:], rot[:], ALU.add))
                    qw = [(SEM["sC"], crs2)]
                    if tp_last.get(2 * sub + iq) is not None:
                        qw.append((SEM["sP"], tp_last[2 * sub + iq]))
                    vqp = p.op("vector", lambda v, buf=buf: v.tensor_tensor(
                        buf[:].rearrange("q (hh f) -> q hh f", hh=8),
                        t1s[:].rearrange("q (hh f) -> q hh f", hh=8),
                        ssum[:, 0:8].unsqueeze(2).broadcast_to([128, 8, 64]), ALU.mult),
                        waits=qw, inc="sV")
                    st_["vqp"] = vqp
                    pmarks[sub].append((vqp, dst, buf, iq))
                qk_psum_free.append(vev)

            def emit_T(sub):
                st_ = ia_state
                for vqp, dst, buf, iq in pmarks[sub]:
                    for pb_ in range(4):
                        pt = PTAp if st_["ppar"] == 0 else PTBp
                        st_["ppar"] ^= 1
                        tw2 = [(SEM["sV"], vqp)]
                        if st_["evq"]:
                            tw2.append((SEM["sC"], st_["evq"][0]))
                        tp = p.op("tensor", lambda t, pb_=pb_, pt=pt, buf=buf: t.transpose(
                            pt[:, 0:128], buf[:, 128 * pb_:128 * (pb_ + 1)], eye_b[:]),
                            waits=tw2, inc="sP")
                        ev = p.op("scalar", lambda sc, dst=dst, pb_=pb_, sub=sub,
                                  pt=pt: sc.copy(
                            dst[:, 512 * pb_ + 128 * sub:512 * pb_ + 128 * (sub + 1)],
                            pt[:, 0:128]), waits=[(SEM["sP"], tp)], inc="sC")
                        st_["evq"].append(ev)
                        if len(st_["evq"]) > 2:
                            st_["evq"].pop(0)
                        tp_last[2 * sub + iq] = tp

            m0, q0 = emit_gen(0)
            emit_proc(0, m0, q0)
            m1_, q1 = emit_gen(1)
            emit_proc(1, m1_, q1)
            emit_T(0)
            m2, q2 = emit_gen(2)
            emit_proc(2, m2, q2)
            emit_T(1)
            m3, q3 = emit_gen(3)
            emit_proc(3, m3, q3)
            emit_T(2)
            emit_T(3)
            qk_done = [qk_psum_free[-1]]
            win_qk = p.cnt["sC"]

            zpair = []
            ZAe = PSA[0:65, 1024:1536]
            ZAo = PSB[0:65, 1024:1536]

            def emit_zdiv(pv_pair, pr_):
                vlast = None
                trb_pv = None
                for e in range(2):
                    ZA_ = ZAe if e == 0 else ZAo
                    ZRB_ = PSA[0:64, 1024:1536] if e == 0 else PSB[0:64, 1024:1536]
                    rw = [(SEM["sP"], pv_pair[e])]
                    if trb_pv is not None:
                        rw.append((SEM["sP"], trb_pv))
                    p.op("vector", lambda v, ZA_=ZA_: v.reciprocal(irl[:], ZA_[64:65, :]),
                         waits=rw)
                    vz2 = p.op("vector", lambda v, ZA_=ZA_: v.tensor_copy(
                        izs[:], ZA_[0:64, :]), inc="sV")
                    trb2 = p.op("tensor", lambda t, ZRB_=ZRB_: t.matmul(
                        ZRB_, ones_r[0:1, 0:64], irl[:], start=True, stop=True),
                        waits=[(SEM["sV"], vz2)], inc="sP")
                    trb_pv = trb2
                    vlast = p.op("vector", lambda v, e=e, pr_=pr_, ZRB_=ZRB_: v.tensor_tensor(
                        zfm[64 * e:64 * (e + 1), 512 * pr_:512 * (pr_ + 1)], izs[:],
                        ZRB_, ALU.mult),
                        waits=[(SEM["sP"], trb2)], inc="sV")
                    ia_z["zg"][e] = vlast
                zpair.append(vlast)

            for pr in range(4):
                pvdone = []
                for sub in range(4):
                    if w > 0:
                        n1 = 512 - 128 * sub
                        n2 = 128 * (sub + 1)
                    else:
                        n1 = 0
                        n2 = 128 * (sub + 1)
                    N = n1 + n2
                    nch = N // 128
                    scw = win_qk if ia_z["last_exp"] is None else \
                        max(win_qk, ia_z["last_exp"])
                    stw = [(SEM["sC"], scw),
                           (SEM["sV"], qk_psum_free[-1])]

                    def mmS(t, pr=pr, sub=sub, n1=n1, n2=n2, w=w,
                            kprev=kprev, kcur=kcur):
                        N_ = n1 + n2
                        for e in range(2):
                            PS_ = PSA if e == 0 else PSB
                            lhs = qfm[64 * e:64 * (e + 1),
                                      512 * pr + 128 * sub:512 * pr + 128 * (sub + 1)]
                            if n1 > 0:
                                t.matmul(PS_[:, 0:n1], lhs,
                                         kprev[64 * e:64 * (e + 1),
                                               512 * pr + 512 - n1:512 * pr + 512],
                                         start=True, stop=False)
                                if n1 < 512:
                                    t.matmul(PS_[:, n1:512], lhs,
                                             kcur[64 * e:64 * (e + 1),
                                                  512 * pr:512 * pr + 512 - n1],
                                             start=False, stop=False)
                                t.matmul(PS_[:, 512:N_], lhs,
                                         kcur[64 * e:64 * (e + 1),
                                              512 * pr + 512 - n1:512 * pr + n2],
                                         start=True, stop=False)
                                t.matmul(PS_[:, 0:128], mk_lt[:], negeye[:],
                                         start=False, stop=False)
                            else:
                                t.matmul(PS_[:, 0:N_], lhs,
                                         kcur[64 * e:64 * (e + 1), 512 * pr:512 * pr + n2],
                                         start=True, stop=False)
                            ins = t.matmul(PS_[:, N_ - 128:N_], mk_gt[:], negeye[:],
                                           start=False, stop=True)
                        return ins
                    ms = p.op("tensor", mmS, waits=stw, inc="sP")
                    ces = []
                    for e in range(2):
                        PS_ = PSA if e == 0 else PSB
                        nm_ = nme if e == 0 else nmo
                        P_ = Pe if e == 0 else Po
                        vnm2 = p.op("vector", lambda v, PS_=PS_, nm_=nm_, N=N: v.tensor_reduce(
                            nm_[:], PS_[:, 0:N], AX, ALU.max, negate=True),
                            waits=[(SEM["sP"], ms)], inc="sV")
                        cw2 = [(SEM["sV"], vnm2)]
                        if debug:
                            cw2.append(p.w("qD"))
                        ce = p.op("scalar", lambda sc, PS_=PS_, nm_=nm_, P_=P_, N=N:
                                  sc.activation(P_[:, 0:N], PS_[:, 0:N], AF.Exp,
                                                bias=nm_[:]),
                                  waits=cw2, inc="sC")
                        ces.append(ce)
                        if debug and l == 0 and w == 1 and pr == 0 and sub == 1 and e == 0:
                            p.op("sync", lambda s_, N=N: s_.dma_start(
                                d_dbgPP[:, 0:N], Pe[:, 0:N]),
                                waits=[(SEM["sC"], ce)], inc="qD", inck=16)

                            p.op("vector", lambda v, PS_=PS_, nm_=nm_: v.tensor_copy(
                                sqs[:, 0:1], nm_[:]), waits=[(SEM["sC"], ce)])
                            p.op("vector", lambda v, PS_=PS_: v.tensor_copy(
                                sqs[:], PS_[:, 0:512]), waits=[(SEM["sC"], ce)])
                            vds = p.op("vector", lambda v: v.tensor_copy(
                                sqs[0:1, 0:1], sqs[0:1, 0:1]), inc="sV")
                            p.op("gpsimd", lambda g_: g_.dma_start(
                                d_dbgKI[:, 2560:3072], sqs[:]),
                                waits=[(SEM["sV"], vds)], inc="qD", inck=16)
                    ia_z["last_exp"] = ces[-1]
                    evs = []
                    for e in range(2):
                        P_ = Pe if e == 0 else Po
                        pt = PTAp if e == 0 else PTBp
                        PT_ = PTe if e == 0 else PTo
                        tp = None
                        for j in range(nch):
                            tw2 = [(SEM["sC"], ces[e])] if j == 0 else []
                            tp = p.op("tensor", lambda t, P_=P_, j=j, pt=pt: t.transpose(
                                pt[:, 128 * j:128 * (j + 1)], P_[:, 128 * j:128 * (j + 1)],
                                eye_b[:]), waits=tw2, inc="sP")
                        evs.append(p.op("scalar", lambda sc, PT_=PT_, pt=pt, nch=nch:
                                        sc.copy(PT_[:, 0:128 * nch], pt[:, 0:128 * nch]),
                                        waits=[(SEM["sP"], tp)], inc="sC"))
                    if sub == 0 and pr > 0:
                        emit_zdiv(ia_z["pv2"], pr - 1)
                    for e in range(2):
                        PT_ = PTe if e == 0 else PTo
                        ZA_ = ZAe if e == 0 else ZAo

                        def mmPV(t, e=e, pr=pr, sub=sub, nch=nch, w=w,
                                 PT_=PT_, ZA_=ZA_, Vprev=Vprev, Vcur=Vcur):
                            for j in range(nch):
                                if w > 0:
                                    comb = 128 * sub + 128 * j
                                    if comb < 512:
                                        vsrc = Vprev
                                        chunk = comb // 128
                                    else:
                                        vsrc = Vcur
                                        chunk = (comb - 512) // 128
                                else:
                                    vsrc = Vcur
                                    chunk = j
                                hh = 2 * pr + e
                                ins = t.matmul(
                                    ZA_[0:65, 128 * sub:128 * (sub + 1)],
                                    vsrc[:, 66 * (8 * chunk + hh):66 * (8 * chunk + hh) + 65],
                                    PT_[:, 128 * j:128 * (j + 1)],
                                    start=(j == 0), stop=(j == nch - 1))
                            return ins
                        pvw = [(SEM["sC"], evs[e])]
                        if sub == 0 and ia_z["zg"][e] is not None:
                            pvw.append((SEM["sV"], ia_z["zg"][e]))
                        pvdone.append(p.op("tensor", mmPV, waits=pvw, inc="sP"))
                ia_z["pv2"] = (pvdone[-2], pvdone[-1])
            emit_zdiv(ia_z["pv2"], 3)

            if debug and l == 0 and w == 1:
                p.op("sync", lambda s_: s_.dma_start(d_dbgQF[:], qfm[:]),
                     waits=[(SEM["sV"], zpair[-1])], inc="qD", inck=16)
                p.op("sync", lambda s_, kcur=kcur: s_.dma_start(d_dbgKF[:], kcur[:]),
                     inc="qD", inck=16)
                p.op("sync", lambda s_, kprev=kprev: s_.dma_start(d_dbgKP[:], kprev[:]),
                     inc="qD", inck=16)
                p.op("sync", lambda s_, Vcur=Vcur: s_.dma_start(d_dbgVC[:], Vcur[:]),
                     inc="qD", inck=16)
                p.op("sync", lambda s_: s_.dma_start(d_dbgZF[:], zfm[:]), inc="qD", inck=16)
            wo_done = []
            for dc in range(NC):
                WO = PSA if dc % 2 == 0 else PSB
                tw = [(SEM["sV"], zpair[-1])]
                if len(wo_done) >= 2:
                    tw.append((SEM["sV"], wo_done[-2][0]))

                def mmwo(t, dc=dc, WO=WO):
                    for ic in range(4):
                        ins = t.matmul(WO[:, 0:512],
                                       iwo[:, 1024 * ic + 128 * dc:1024 * ic + 128 * (dc + 1)],
                                       zfm[:, 512 * ic:512 * (ic + 1)],
                                       start=(ic == 0), stop=(ic == 3))
                    return ins
                mo = p.op("tensor", mmwo, waits=tw, inc="sP")
                vw = [(SEM["sP"], mo), hldw, p.w("qS3"), p.w("qHG")]
                vy1 = p.op("vector", lambda v, dc=dc, WO=WO: v.tensor_tensor(
                    y1e[dc % 2][:], WO[:, 0:512], yt[dc][:], ALU.add),
                    waits=vw, inc="sV")
                p.op("sync", lambda s, dc=dc, w=w: s.dma_start(
                    yout[128 * dc:128 * (dc + 1), 512 * w:512 * (w + 1)], y1e[dc % 2][:]),
                    waits=[(SEM["sV"], vy1)], inc="qS3", inck=16)
                p.op("gpsimd", lambda g, dc=dc, w=w: g.dma_start(
                    hout[128 * dc:128 * (dc + 1), 512 * w:512 * (w + 1)], y1e[dc % 2][:]),
                    waits=[(SEM["sV"], vy1)], inc="qHG", inck=16)
                wo_done.append((vy1, mo))
            ia_done_prev_w = (wo_done[-1][1], wo_done[-1][0])

        stackIA.close()

        if debug:
            p.op("sync", lambda s_, l=l: s_.dma_start(d_dbgY[2 * l][:], yout[:]),
                 waits=[p.w("qS3"), p.w("qHG")], inc="qD", inck=16)
        stackF = ExitStack()
        w1s = stackF.enter_context(nc.sbuf_tensor(f"l{l}_f_w1", [128, NC * FF], BF16))
        w2s = stackF.enter_context(nc.sbuf_tensor(f"l{l}_f_w2", [128, NHC * D], BF16))
        gsb = stackF.enter_context(nc.sbuf_tensor(f"l{l}_f_g", [128, 8], F32))
        bsb = stackF.enter_context(nc.sbuf_tensor(f"l{l}_f_b", [128, 8], F32))
        b1as = stackF.enter_context(nc.sbuf_tensor(f"l{l}_f_b1a", [128, 22], F32))
        b1gs = stackF.enter_context(nc.sbuf_tensor(f"l{l}_f_b1g", [128, 22], F32))
        b2s = stackF.enter_context(nc.sbuf_tensor(f"l{l}_f_b2", [128, 8], F32))
        fht = [stackF.enter_context(nc.sbuf_tensor(f"l{l}_f_h{c}", [128, 512], BF16))
               for c in range(NC)]
        fyt = [stackF.enter_context(nc.sbuf_tensor(f"l{l}_f_y{c}", [128, 512], F32))
               for c in range(2)]
        fhh = [stackF.enter_context(nc.sbuf_tensor(f"l{l}_f_hh{i}", [128, 512], BF16))
               for i in range(2)]
        fhn2 = [[stackF.enter_context(nc.sbuf_tensor(f"l{l}_f_hn{gb}_{c}", [128, 512], BF16))
                 for c in range(NC)] for gb in range(2)]
        fstat = stackF.enter_context(nc.sbuf_tensor(f"l{l}_f_stat", [1, 2048], F32))
        fts = stackF.enter_context(nc.sbuf_tensor(f"l{l}_f_ts", [128, 512], F32))
        fga = [stackF.enter_context(nc.sbuf_tensor(f"l{l}_f_ga{i}", [128, 512], F32))
               for i in range(2)]
        fav = [stackF.enter_context(nc.sbuf_tensor(f"l{l}_f_av{i}", [128, 512], F32))
               for i in range(2)]
        fm = stackF.enter_context(nc.sbuf_tensor(f"l{l}_f_m", [128, NHC * 512], BF16))
        fy2 = [stackF.enter_context(nc.sbuf_tensor(f"l{l}_f_y2{i}", [128, 512], F32))
               for i in range(2)]
        fS1 = stackF.enter_context(nc.psum_tensor(f"l{l}_f_S1", [1, 512], F32))
        fS2 = stackF.enter_context(nc.psum_tensor(f"l{l}_f_S2", [1, 512], F32))
        fMB = stackF.enter_context(nc.psum_tensor(f"l{l}_f_MB", [128, 512], F32))
        fRB = stackF.enter_context(nc.psum_tensor(f"l{l}_f_RB", [128, 512], F32))
        fA = [stackF.enter_context(nc.psum_tensor(f"l{l}_f_A{i}", [128, 512], F32))
              for i in range(2)]
        fG = [stackF.enter_context(nc.psum_tensor(f"l{l}_f_G{i}", [128, 512], F32))
              for i in range(2)]

        barF = p.barrier()
        p.op("gpsimd", lambda g, l=l: g.dma_start(
            w1s[:].rearrange("q (c f) -> q c f", c=NC),
            d_iw1[l][:].rearrange("(c q) f -> q c f", q=128)),
            waits=barF("gpsimd"), inc="qWG", inck=16)
        p.op("gpsimd", lambda g, l=l: g.dma_start(
            w2s[:].rearrange("q (c f) -> q c f", c=NHC),
            d_iw2[l][:].rearrange("(c q) f -> q c f", q=128)), inc="qWG", inck=16)
        fwld = p.w("qWG")
        for dst, src in ((gsb, d_g[l]), (bsb, d_b[l]), (b1as, d_b1a[l]),
                         (b1gs, d_b1g[l]), (b2s, d_b2[l])):
            p.op("sync", lambda s, dst=dst, src=src: s.dma_start(dst[:], src[:]),
                 waits=barF("sync"), inc="qL", inck=16)
        fsld = p.w("qL")
        p.op("vector", lambda v: v.memset(fm[:, 512 * 21:512 * 22], 0.0),
             waits=barF("vector"))

        prev_hn = None
        prev_w2 = None
        prev_w2_pair = None
        gstates = {}

        def f_loads(G):
            fhn = fhn2[G % 2]
            fmemv = 0
            for c in range(NC):
                tw = [] if (prev_hn is None or c > 0) else [(SEM["sV"], prev_hn)]
                if c < 4 or G > 0:
                    off = 512 * G if c < 4 else 512 * G - 1
                    p.op("sync", lambda s, c=c, off=off, hT_=hout: s.dma_start(
                        fht[c][:], hT_[128 * c:128 * (c + 1), off:off + 512]),
                        waits=tw, inc="qL", inck=16)
                else:
                    fmemv = p.op("vector", lambda v, c=c: v.memset(fht[c][:, 0:1], 0.0),
                                 waits=tw, inc="sV")
                    p.op("sync", lambda s, c=c, hT_=hout: s.dma_start(
                        fht[c][:, 1:512], hT_[128 * c:128 * (c + 1), 0:511]),
                        waits=tw, inc="qL", inck=16)
            gstates[G] = {"fhn": fhn, "fmemv": fmemv, "fld": p.w("qL")}

        def f_stats(G):
            st_ = gstates[G]
            for c in range(NC):
                tw = barF("tensor") + [fwld] if (G == 0 and c == 0) else []
                if c == 0:
                    tw.append(st_["fld"])
                if G == 0 and c == 0:
                    tw.append((SEM["sV"], st_["fmemv"]))
                p.op("tensor", lambda t, c=c: t.matmul(
                    fS1[:], ones_cb[:], fht[c][:], start=(c == 0), stop=(c == NC - 1)),
                    waits=tw, inc="sP")
            sq2 = []
            for c in range(NC):
                tw = [st_["fld"]] if c == 0 else []
                if c >= 2:
                    tw.append((SEM["sP"], sq2[c - 2]))
                vb = p.op("vector", lambda v, c=c: v.tensor_tensor(
                    fhh[c % 2][:], fht[c][:], fht[c][:], ALU.mult), waits=tw, inc="sV")
                sq2.append(p.op("tensor", lambda t, c=c: t.matmul(
                    fS2[:], ones_cb[:], fhh[c % 2][:], start=(c == 0), stop=(c == NC - 1)),
                    waits=[(SEM["sV"], vb)], inc="sP"))
            p.op("vector", lambda v: v.tensor_scalar_mul(fstat[0:1, 0:512], fS1[:], -1.0 / D),
                 waits=[(SEM["sP"], sq2[7])])
            p.op("vector", lambda v: v.tensor_scalar_mul(fstat[0:1, 512:1024], fS2[:],
                                                         1.0 / D))
            p.op("vector", lambda v: v.tensor_copy(settle_d[:], settle_s[:]))
            p.op("vector", lambda v: v.tensor_tensor(
                fstat[0:1, 1024:1536], fstat[0:1, 0:512], fstat[0:1, 0:512], ALU.mult))
            p.op("vector", lambda v: v.tensor_copy(settle_d[:], settle_s[:]))
            vv2 = p.op("vector", lambda v: v.tensor_tensor(
                fstat[0:1, 1024:1536], fstat[0:1, 512:1024], fstat[0:1, 1024:1536],
                ALU.subtract), inc="sV")
            cw = barF("scalar") if G == 0 else []
            p.op("scalar", lambda sc: sc.activation(fstat[0:1, 1536:2048],
                                                    fstat[0:1, 1024:1536], AF.Ln, bias=1e-5),
                 waits=cw + [(SEM["sV"], vv2)])
            st_["crs3"] = p.op("scalar", lambda sc: sc.activation(
                fstat[0:1, 1536:2048], fstat[0:1, 1536:2048], AF.Exp, scale=-0.5), inc="sC")

        def f_bcast_hn(G):
            nonlocal prev_hn
            st_ = gstates[G]
            fhn = st_["fhn"]
            p.op("tensor", lambda t: t.matmul(fMB[:], ones_r[:], fstat[0:1, 0:512],
                                              start=True, stop=True),
                 waits=[(SEM["sC"], st_["crs3"])], inc="sP")
            mb2 = p.op("tensor", lambda t: t.matmul(fRB[:], ones_r[:], fstat[0:1, 1536:2048],
                                                    start=True, stop=True), inc="sP")
            hn_done = []
            for c in range(NC):
                nw2 = [(SEM["sP"], mb2)]
                if c == 0 and G >= 1:
                    nw2.append((SEM["sP"], gstates[G - 1]["w1m0"]))
                p.op("vector", lambda v, c=c: v.tensor_tensor(
                    fts[:], fht[c][:], fMB[:], ALU.add), waits=nw2)
                p.op("vector", lambda v: v.tensor_tensor(fts[:], fts[:], fRB[:], ALU.mult))
                hn_done.append(p.op("vector", lambda v, c=c, fhn=fhn: v.tensor_scalar(
                    fhn[c][:], fts[:], gsb[:, c:c + 1], bsb[:, c:c + 1],
                    ALU.mult, ALU.add), inc="sV"))
            st_["hnw"] = (SEM["sV"], hn_done[-1])
            st_["hnl"] = hn_done[-1]
            prev_hn = hn_done[-1]

        def f_w1(G):
            st_ = gstates[G]
            fhn = st_["fhn"]
            mdone = []
            for i in range(NHC):
                wa = 128 if i < 21 else 42
                Ap, Gp = fA[i % 2], fG[i % 2]
                tw = [st_["hnw"]]
                if i < 2 and prev_w2_pair is not None:
                    tw.append((SEM["sV"], prev_w2_pair[i]))
                if len(mdone) >= 2:
                    tw.append((SEM["sV"], mdone[-2]))

                def mma(t, i=i, wa=wa, Ap=Ap, Gp=Gp, fhn=fhn):
                    for c in range(NC):
                        t.matmul(Ap[0:wa, 0:512],
                                 w1s[:, FF * c + 128 * i:FF * c + 128 * i + wa],
                                 fhn[c][:], start=(c == 0), stop=(c == NC - 1))
                    for c in range(NC):
                        ins = t.matmul(Gp[0:wa, 0:512],
                                       w1s[:, FF * c + HID + 128 * i:
                                           FF * c + HID + 128 * i + wa],
                                       fhn[c][:], start=(c == 0), stop=(c == NC - 1))
                    return ins
                mw1 = p.op("tensor", mma, waits=tw, inc="sP")
                if i == 0:
                    st_["w1m0"] = mw1
                cg = p.op("scalar", lambda sc, i=i, wa=wa, Gp=Gp: sc.activation(
                    fga[i % 2][0:wa, :], Gp[0:wa, 0:512], AF.Gelu, bias=b1gs[0:wa, i:i + 1]),
                    waits=[(SEM["sP"], mw1)], inc="sC")
                va = p.op("vector", lambda v, i=i, wa=wa, Ap=Ap: v.tensor_scalar_add(
                    fav[i % 2][0:wa, :], Ap[0:wa, 0:512], b1as[0:wa, i:i + 1]),
                    waits=[(SEM["sP"], mw1)], inc="sV")
                mdone.append(p.op("vector", lambda v, i=i, wa=wa: v.tensor_tensor(
                    fm[0:wa, 512 * i:512 * (i + 1)], fav[i % 2][0:wa, :],
                    fga[i % 2][0:wa, :], ALU.mult),
                    waits=[(SEM["sC"], cg)], inc="sV"))
            st_["mw"] = (SEM["sV"], mdone[-1])
            st_["w2_done"] = []
            for dc in range(2):
                yw = [(SEM["sV"], st_["hnl"])]
                if prev_w2 is not None:
                    yw.append((SEM["sV"], prev_w2))
                p.op("sync", lambda s, dc=dc, G=G, yT_=yout: s.dma_start(
                    fyt[dc][:], yT_[128 * dc:128 * (dc + 1), 512 * G:512 * (G + 1)]),
                    waits=yw, inc="qY", inck=16)

        def f_w2(G, dclo, dchi):
            nonlocal prev_w2, prev_w2_pair
            st_ = gstates[G]
            w2_done = st_["w2_done"]
            for dc in range(dclo, dchi):
                W2 = fA[dc % 2]
                tw = [st_["mw"]]
                if len(w2_done) >= 2:
                    tw.append((SEM["sV"], w2_done[-2][0]))

                def mmw2(t, dc=dc, W2=W2):
                    for hc in range(NHC):
                        ins = t.matmul(W2[:, 0:512],
                                       w2s[:, D * hc + 128 * dc:D * hc + 128 * (dc + 1)],
                                       fm[:, 512 * hc:512 * (hc + 1)],
                                       start=(hc == 0), stop=(hc == NHC - 1))
                    return ins
                mo2 = p.op("tensor", mmw2, waits=tw, inc="sP")
                vy2 = p.op("vector", lambda v, dc=dc, W2=W2: v.scalar_tensor_tensor(
                    fy2[dc % 2][:], W2[:, 0:512], b2s[:, dc:dc + 1], fyt[dc % 2][:],
                    ALU.add, ALU.add),
                    waits=[(SEM["sP"], mo2), p.w("qY"), p.w("qS3"), p.w("qHG")], inc="sV")
                if dc + 2 < NC:
                    p.op("sync", lambda s, dc=dc, G=G, yT_=yout: s.dma_start(
                        fyt[dc % 2][:], yT_[128 * (dc + 2):128 * (dc + 3),
                                            512 * G:512 * (G + 1)]),
                        waits=[(SEM["sV"], vy2)], inc="qY", inck=16)
                p.op("sync", lambda s, dc=dc, G=G, yT_=yin: s.dma_start(
                    yT_[128 * dc:128 * (dc + 1), 512 * G:512 * (G + 1)], fy2[dc % 2][:]),
                    waits=[(SEM["sV"], vy2)], inc="qS3", inck=16)
                if l == 0:
                    tw2 = barF("gpsimd") if (G == 0 and dc == 0) else []
                    p.op("gpsimd", lambda g, dc=dc, G=G, hT_=hin: g.dma_start(
                        hT_[128 * dc:128 * (dc + 1), 512 * G:512 * (G + 1)], fy2[dc % 2][:]),
                        waits=tw2 + [(SEM["sV"], vy2)], inc="qHG", inck=16)
                w2_done.append((vy2, mo2))
            if dchi == NC:
                prev_w2 = w2_done[-1][0]
                prev_w2_pair = (w2_done[-2][0], w2_done[-1][0])

        for G in range(8):
            f_loads(G)
            if G > 0:
                f_w2(G - 1, 0, 4)
            f_stats(G)
            if G > 0:
                f_w2(G - 1, 4, 6)
            f_bcast_hn(G)
            if G > 0:
                f_w2(G - 1, 6, 8)
            f_w1(G)
        f_w2(7, 0, 8)


        if debug:
            p.op("sync", lambda s_, l=l: s_.dma_start(d_dbgY[2 * l + 1][:], yin[:]),
                 waits=[p.w("qS3"), p.w("qHG")], inc="qD", inck=16)
        stackF.close()

    for l in range(2):
        _build_layer(l)

    stackC = ExitStack()
    c_a = [stackC.enter_context(nc.sbuf_tensor(f"c_a{i}", [128, D], F32)) for i in range(2)]
    c_b = [stackC.enter_context(nc.sbuf_tensor(f"c_b{i}", [128, D], F32)) for i in range(2)]
    c_s = [stackC.enter_context(nc.sbuf_tensor(f"c_s{i}", [128, D], F32)) for i in range(2)]
    c_o = [stackC.enter_context(nc.sbuf_tensor(f"c_o{i}", [128, D], F32)) for i in range(2)]
    c_pt = [stackC.enter_context(nc.psum_tensor(f"c_pt{i}", [128, 128], F32)) for i in range(2)]

    barC = p.barrier()
    cst = []
    cev = {}
    for i in range(NT):
        b = i % 2
        tw = barC("sync") if i == 0 else []
        if i >= 1:
            tw.append((SEM["sV"], cev[(i - 1, "sum")]))
        if i >= 2:
            tw.append((SEM["sV"], cev[(i - 2, "sum")]))
        p.op("sync", lambda s, i=i, b=b: s.dma_start(
            c_a[b][:].rearrange("q (c f) -> q c f", c=NC),
            d_x1T[:].rearrange("(c q) s_ -> q c s_", q=128)[:, :, 128 * i:128 * (i + 1)]),
            waits=tw, inc="qL", inck=16)
        p.op("sync", lambda s, i=i, b=b: s.dma_start(
            c_b[b][:].rearrange("q (c f) -> q c f", c=NC),
            d_yT[0][:].rearrange("(c q) s_ -> q c s_", q=128)[:, :, 128 * i:128 * (i + 1)]),
            inc="qL", inck=16)
        ldw2 = p.w("qL")
        tw = barC("vector") if i == 0 else []
        if i >= 2:
            tw.append((SEM["sP"], cev[(i - 2, "tp")]))
        vs = p.op("vector", lambda v, b=b: v.tensor_tensor(c_s[b][:], c_a[b][:], c_b[b][:],
                                                           ALU.add),
                  waits=tw + [ldw2], inc="sV")
        cev[(i, "sum")] = vs
        for c in range(NC):
            tw = barC("tensor") if (i == 0 and c == 0) else []
            if (i, c - 2) in cev:
                tw.append((SEM["sV"], cev[(i, c - 2)]))
            elif (i - 1, c + 6) in cev:
                tw.append((SEM["sV"], cev[(i - 1, c + 6)]))
            tp = p.op("tensor", lambda t, b=b, c=c: t.transpose(
                c_pt[c % 2][:], c_s[b][:, 128 * c:128 * (c + 1)], eye_f[:]),
                waits=tw + [(SEM["sV"], vs)], inc="sP")
            if c == NC - 1:
                cev[(i, "tp")] = tp
            ew = [(SEM["sP"], tp)]
            if c == 0 and i >= 2:
                ew.append(p.w("qS4"))
            cev[(i, c)] = p.op("vector", lambda v, b=b, c=c: v.tensor_copy(
                c_o[b][:, 128 * c:128 * (c + 1)], c_pt[c % 2][:]), waits=ew, inc="sV")
        cst.append(p.op("sync", lambda s, i=i, b=b: s.dma_start(
            d_out[128 * i:128 * (i + 1), :], c_o[b][:]),
            waits=[(SEM["sV"], cev[(i, NC - 1)])], inc="qS4", inck=16))

    p.op("sync", None, waits=[p.w(n) for n in
                              ("qS4", "qS3", "qS2", "qS1", "qZ", "qL", "qY", "qWG", "qHG",
                               "qD")])
    stackC.close()
    stack.close()

    p.emit()
    return nc


_NC_CACHE = None


def _host_tables():
    hi = np.arange(0, 64, 2, dtype=np.float32)
    invf = (np.float32(1.0) / np.power(np.float32(10000.0), hi / np.float32(64.0))
            ).astype(np.float32)
    t = np.arange(S, dtype=np.float32)
    freqs = t[:, None] * invf[None, :]
    freqs = np.concatenate([freqs, freqs], axis=1).astype(np.float32)
    base = ((hi + np.float32(0.4 * 64)) / np.float32(1.4 * 64)).astype(np.float32)
    power = ((t - np.float32(S // 2)) / np.float32(256.0)).astype(np.float32)
    scale = np.power(base[None, :], power[:, None]).astype(np.float32)
    scale = np.concatenate([scale, scale], axis=1).astype(np.float32)
    cos = np.cos(freqs).astype(np.float32)
    sin = np.sin(freqs).astype(np.float32)
    return (np.ascontiguousarray(cos * scale), np.ascontiguousarray(sin * scale),
            np.ascontiguousarray(cos / scale), np.ascontiguousarray(sin / scale))


def kernel(L0_attn_x, L0_attn_MhWqkv, L0_ffn_wqkv, L0_ffn_q_scale, L0_ffn_k_scale,
           L0_ffn_wo, L0_ffn_ln_g, L0_ffn_ln_b, L0_ffn_w1, L0_ffn_b1, L0_ffn_w2,
           L0_ffn_b2):
    global _NC_CACHE
    if _NC_CACHE is None:
        _NC_CACHE = build_nc()
    nc = _NC_CACHE

    f32 = lambda a: np.ascontiguousarray(np.asarray(a, dtype=np.float32))
    r = f32(L0_attn_MhWqkv).reshape(D, 16, 3, 64)
    wq = np.ascontiguousarray(r[:, :, 0, :].reshape(D, D))
    wk = np.ascontiguousarray(r[:, :, 1, :].reshape(D, D))
    wv = np.ascontiguousarray(r[:, :, 2, :].reshape(D, D))
    tCS, tSS, tCiS, tSiS = _host_tables()

    shared = {"wq": wq, "wk": wk, "wv": wv,
              "tCS": tCS, "tSS": tSS, "tCiS": tCiS, "tSiS": tSiS}
    for l in range(2):
        shared[f"iwqkv{l}"] = f32(L0_ffn_wqkv[l])
        shared[f"iwo{l}"] = f32(L0_ffn_wo[l])
        shared[f"iw1{l}"] = f32(L0_ffn_w1[l])
        w2p = np.zeros((HIDP, D), dtype=np.float32)
        w2p[:HID] = f32(L0_ffn_w2[l])
        shared[f"iw2{l}"] = w2p
        shared[f"qs{l}"] = f32(L0_ffn_q_scale[l]).reshape(1, 64)
        shared[f"ks{l}"] = f32(L0_ffn_k_scale[l]).reshape(1, 64)
        shared[f"g{l}"] = np.ascontiguousarray(f32(L0_ffn_ln_g[l]).reshape(8, 128).T)
        shared[f"bb{l}"] = np.ascontiguousarray(f32(L0_ffn_ln_b[l]).reshape(8, 128).T)
        b1 = f32(L0_ffn_b1[l])
        b1a = np.zeros(HIDP, dtype=np.float32)
        b1a[:HID] = b1[:HID]
        b1g = np.zeros(HIDP, dtype=np.float32)
        b1g[:HID] = b1[HID:]
        shared[f"b1a{l}"] = np.ascontiguousarray(b1a.reshape(22, 128).T)
        shared[f"b1g{l}"] = np.ascontiguousarray(b1g.reshape(22, 128).T)
        shared[f"b2{l}"] = np.ascontiguousarray(f32(L0_ffn_b2[l]).reshape(8, 128).T)

    xs = f32(L0_attn_x)
    in_maps = []
    for c in range(8):
        m = dict(shared)
        m["x"] = np.ascontiguousarray(xs[c])
        in_maps.append(m)

    res = run_bass_kernel_spmd(nc, in_maps, core_ids=list(range(8)))
    out = np.stack([res.results[c]["out"] for c in range(8)], axis=0)
    return out.astype(np.float32)


# version 10
# speedup vs baseline: 1.1988x; 1.1988x over previous
from contextlib import ExitStack

import numpy as np
import concourse.bass as bass
import concourse.mybir as mybir
from concourse.bass_utils import run_bass_kernel_spmd

F32 = mybir.dt.float32
BF16 = mybir.dt.bfloat16
AF = mybir.ActivationFunctionType
ALU = mybir.AluOpType
AX = mybir.AxisListType.X

S = 4096
D = 1024
NT = S // 128
NC = D // 128
HID = 2730
HIDP = 2816
FF = 5460
NHC = 22
LN8 = float(np.log(8.0))
NEGBIG = -1.0e9
ENGS = ("tensor", "scalar", "vector", "gpsimd", "sync")


class Prog:
    def __init__(self, nc):
        self.nc = nc
        self.ops = {e: [] for e in ENGS}
        self.sem = {}
        self.cnt = {}

    def mksem(self, name):
        self.sem[name] = self.nc.alloc_semaphore(name)
        self.cnt[name] = 0

    def op(self, eng, fn, waits=(), inc=None, inck=1):
        self.ops[eng].append((fn, tuple(waits), inc, inck))
        if inc is not None:
            self.cnt[inc] += inck
            return self.cnt[inc]
        return None

    def w(self, name):
        return (self.sem[name], self.cnt[name])

    def snap(self):
        return {n: self.cnt[n] for n in self.sem}

    def barrier(self):
        waits = [(self.sem[n], self.cnt[n]) for n in self.sem]
        used = {e: False for e in ENGS}

        def first(eng, extra=()):
            if not used[eng]:
                used[eng] = True
                return list(waits) + list(extra)
            return list(extra)
        return first

    def emit(self):
        with self.nc.Block() as blk:
            for eng in ENGS:
                if not self.ops[eng]:
                    continue

                def body(e, _ops=tuple(self.ops[eng])):
                    for fn, waits, inc, inck in _ops:
                        for sem, val in waits:
                            if val > 0:
                                e.wait_ge(sem, val)
                        ins = fn(e) if fn is not None else None
                        if inc is not None:
                            ins.then_inc(self.sem[inc], inck)

                getattr(blk, eng)(body)


def build_nc(debug=False):
    nc = bass.Bass("TRN2", target_bir_lowering=False)
    p = Prog(nc)
    for s_ in ("qWG", "qHG", "qL", "qY", "qZ", "qS1", "qS2", "qS3", "qS4", "qD",
               "sP", "sV", "sC", "sL"):
        p.mksem(s_)
    SEM = p.sem

    d_x = nc.dram_tensor("x", [S, D], F32, kind="ExternalInput")
    d_wq = nc.dram_tensor("wq", [D, D], F32, kind="ExternalInput")
    d_wk = nc.dram_tensor("wk", [D, D], F32, kind="ExternalInput")
    d_wv = nc.dram_tensor("wv", [D, D], F32, kind="ExternalInput")
    d_iwqkv = [nc.dram_tensor(f"iwqkv{l}", [D, 1536], F32, kind="ExternalInput") for l in range(2)]
    d_iwo = [nc.dram_tensor(f"iwo{l}", [512, D], F32, kind="ExternalInput") for l in range(2)]
    d_iw1 = [nc.dram_tensor(f"iw1{l}", [D, FF], F32, kind="ExternalInput") for l in range(2)]
    d_iw2 = [nc.dram_tensor(f"iw2{l}", [HIDP, D], F32, kind="ExternalInput") for l in range(2)]
    d_qs = [nc.dram_tensor(f"qs{l}", [1, 64], F32, kind="ExternalInput") for l in range(2)]
    d_ks = [nc.dram_tensor(f"ks{l}", [1, 64], F32, kind="ExternalInput") for l in range(2)]
    d_g = [nc.dram_tensor(f"g{l}", [128, 8], F32, kind="ExternalInput") for l in range(2)]
    d_b = [nc.dram_tensor(f"bb{l}", [128, 8], F32, kind="ExternalInput") for l in range(2)]
    d_b1a = [nc.dram_tensor(f"b1a{l}", [128, 22], F32, kind="ExternalInput") for l in range(2)]
    d_b1g = [nc.dram_tensor(f"b1g{l}", [128, 22], F32, kind="ExternalInput") for l in range(2)]
    d_b2 = [nc.dram_tensor(f"b2{l}", [128, 8], F32, kind="ExternalInput") for l in range(2)]
    d_tab = [nc.dram_tensor(nm, [S, 64], F32, kind="ExternalInput")
             for nm in ("tCS", "tSS", "tCiS", "tSiS")]
    d_out = nc.dram_tensor("out", [S, D], F32, kind="ExternalOutput")

    kd = "ExternalOutput" if debug else "Internal"
    if debug:
        d_dbgQ = nc.dram_tensor("dbgQ", [128, S], BF16, kind="ExternalOutput")
        d_dbgK = nc.dram_tensor("dbgK", [128, S], BF16, kind="ExternalOutput")
        d_dbgV = nc.dram_tensor("dbgV", [128, NT * 132], BF16, kind="ExternalOutput")
        d_dbgM = nc.dram_tensor("dbgM", [128, 2], F32, kind="ExternalOutput")
        d_dbgP = nc.dram_tensor("dbgP", [128, 1024], BF16, kind="ExternalOutput")
        d_dbgZ = nc.dram_tensor("dbgZ", [65, 1024], F32, kind="ExternalOutput")
        d_dbgXN = nc.dram_tensor("dbgXN", [128, NC * S], BF16, kind="ExternalOutput")
        d_dbgXNT = nc.dram_tensor("dbgXNT", [S, D], F32, kind="ExternalOutput")
        d_dbgZT2 = nc.dram_tensor("dbgZT2", [D, S], F32, kind="ExternalOutput")
        d_dbgX1L = nc.dram_tensor("dbgX1L", [D, S], F32, kind="ExternalOutput")
        d_dbgY = [nc.dram_tensor(f"dbgY{k}", [D, S], F32, kind="ExternalOutput")
                  for k in range(4)]
        d_dbgQF = nc.dram_tensor("dbgQF", [128, 4 * 512], BF16, kind="ExternalOutput")
        d_dbgKF = nc.dram_tensor("dbgKF", [128, 4 * 512], BF16, kind="ExternalOutput")
        d_dbgVC = nc.dram_tensor("dbgVC", [128, 4 * 8 * 66], BF16, kind="ExternalOutput")
        d_dbgZF = nc.dram_tensor("dbgZF", [128, 4 * 512], BF16, kind="ExternalOutput")
        d_dbgPP = nc.dram_tensor("dbgPP", [128, 640], BF16, kind="ExternalOutput")
        d_dbgKI = nc.dram_tensor("dbgKI", [128, 6 * 512], F32, kind="ExternalOutput")
        d_dbgKP = nc.dram_tensor("dbgKP", [128, 4 * 512], BF16, kind="ExternalOutput")
        d_dbgTB = nc.dram_tensor("dbgTB", [128, 2048], F32, kind="ExternalOutput")
        d_dbgST = nc.dram_tensor("dbgST", [128, 8 * NT], F32, kind="ExternalOutput")
        d_dbgWQ = nc.dram_tensor("dbgWQ", [128, NC * D], BF16, kind="ExternalOutput")
    d_xT = nc.dram_tensor("sc_xT", [D, S], F32, kind=kd)
    d_zT = nc.dram_tensor("sc_zT", [D, S], F32, kind=kd)
    d_x1T = nc.dram_tensor("sc_x1T", [D, S], F32, kind=kd)
    d_yT = [nc.dram_tensor(f"sc_yT{i}", [D, S], F32, kind=kd) for i in range(2)]
    d_hT = [nc.dram_tensor(f"sc_hT{i}", [D, S], BF16, kind=kd) for i in range(2)]

    eye_b = nc.alloc_sbuf_tensor("eye_b", [128, 128], BF16)
    eye_f = nc.alloc_sbuf_tensor("eye_f", [128, 128], F32)
    ones_r = nc.alloc_sbuf_tensor("ones_r", [1, 128], F32)
    ones_cb = nc.alloc_sbuf_tensor("ones_cb", [128, 1], BF16)
    ones_cf = nc.alloc_sbuf_tensor("ones_cf", [128, 1], F32)
    mask_up = nc.alloc_sbuf_tensor("mask_up", [128, 128], F32)
    mask_lo = nc.alloc_sbuf_tensor("mask_lo", [128, 128], F32)
    mk_lt = nc.alloc_sbuf_tensor("mk_lt", [128, 128], BF16)
    mk_gt = nc.alloc_sbuf_tensor("mk_gt", [128, 128], BF16)
    negeye = nc.alloc_sbuf_tensor("negeye", [128, 128], BF16)
    cvals = nc.alloc_sbuf_tensor("cvals", [128, 4], F32)
    settle_s = nc.alloc_sbuf_tensor("settle_s", [128, 128], F32)
    settle_d = nc.alloc_sbuf_tensor("settle_d", [128, 128], F32)
    nc.const_aps.register(F32, 1e-5, cvals[:, 0:1])
    nc.const_aps.register(F32, LN8, cvals[:, 1:2])
    nc.const_aps.register(F32, 1e-30, cvals[:, 3:4])
    nc.const_aps.register(F32, 0.0, cvals[:, 2:3])

    stack = ExitStack()

    def _consts(g):
        g.memset(eye_b[:], 1.0)
        g.affine_select(eye_b[:], eye_b[:], pattern=[[1, 128]], compare_op=ALU.is_equal,
                        fill=0.0, base=0, channel_multiplier=-1)
        g.memset(eye_f[:], 1.0)
        g.affine_select(eye_f[:], eye_f[:], pattern=[[1, 128]], compare_op=ALU.is_equal,
                        fill=0.0, base=0, channel_multiplier=-1)
        g.memset(mask_up[:], 0.0)
        g.affine_select(mask_up[:], mask_up[:], pattern=[[1, 128]], compare_op=ALU.is_ge,
                        fill=NEGBIG, base=0, channel_multiplier=-1)
        g.memset(mask_lo[:], 0.0)
        g.affine_select(mask_lo[:], mask_lo[:], pattern=[[-1, 128]],
                        compare_op=ALU.is_ge, fill=NEGBIG, base=0,
                        channel_multiplier=1)
        g.memset(mk_lt[:], 1.0)
        g.affine_select(mk_lt[:], mk_lt[:], pattern=[[1, 128]], compare_op=ALU.is_ge,
                        fill=0.0, base=-1, channel_multiplier=-1)
        g.memset(mk_gt[:], 1.0)
        g.affine_select(mk_gt[:], mk_gt[:], pattern=[[-1, 128]], compare_op=ALU.is_ge,
                        fill=0.0, base=-1, channel_multiplier=1)
        g.memset(negeye[:], NEGBIG)
        return g.affine_select(negeye[:], negeye[:], pattern=[[1, 128]],
                               compare_op=ALU.is_equal, fill=0.0, base=0,
                               channel_multiplier=-1)

    p.op("gpsimd", _consts, inc="sL")

    def _vconsts(v):
        v.memset(ones_r[:], 1.0)
        v.memset(ones_cb[:], 1.0)
        v.memset(cvals[:, 0:1], 1e-5)
        v.memset(cvals[:, 1:2], LN8)
        v.memset(cvals[:, 2:3], 0.0)
        v.memset(cvals[:, 3:4], 1e-30)
        v.memset(settle_s[:], 0.0)
        return v.memset(ones_cf[:], 1.0)

    cv = p.op("vector", _vconsts, inc="sV")
    consts_v = (SEM["sV"], cv)

    stackA = ExitStack()
    wq_sb = stackA.enter_context(nc.sbuf_tensor("wq_sb", [128, NC * D], BF16))
    wk_sb = stackA.enter_context(nc.sbuf_tensor("wk_sb", [128, NC * D], BF16))
    wv_sb = stackA.enter_context(nc.sbuf_tensor("wv_sb", [128, NC * D], BF16))
    for dsrc, dst in ((d_wq, wq_sb), (d_wk, wk_sb), (d_wv, wv_sb)):
        p.op("gpsimd",
             lambda g, a=dst, b=dsrc: g.dma_start(
                 a[:].rearrange("q (c f) -> q c f", c=NC),
                 b[:].rearrange("(c q) f -> q c f", q=128)),
             inc="qWG", inck=16)
    w_outer = p.w("qWG")

    xn_fm = [stackA.enter_context(nc.sbuf_tensor(f"xn_fm{c}", [128, S], BF16))
             for c in range(NC)]

    stackA0 = ExitStack()
    a0_x = [stackA0.enter_context(nc.sbuf_tensor(f"a0_x{i}", [128, D], F32)) for i in range(2)]
    a0_xn = stackA0.enter_context(nc.sbuf_tensor("a0_xn", [128, D], F32))
    a0_xnb = [stackA0.enter_context(nc.sbuf_tensor(f"a0_xnb{i}", [128, D], BF16))
              for i in range(2)]
    a0_xTt = [stackA0.enter_context(nc.sbuf_tensor(f"a0_xTt{i}", [128, D], F32))
              for i in range(2)]
    a0_sq = stackA0.enter_context(nc.sbuf_tensor("a0_sq", [128, D], F32))
    a0_st = stackA0.enter_context(nc.sbuf_tensor("a0_st", [128, 8], F32))
    a0_ptb = [stackA0.enter_context(nc.psum_tensor(f"a0_ptb{i}", [128, 128], BF16))
              for i in range(2)]
    a0_ptf = [stackA0.enter_context(nc.psum_tensor(f"a0_ptf{i}", [128, 128], F32))
              for i in range(2)]

    tile_done = []
    vstats = []
    bevac = {}
    fevac = {}
    for i in range(NT):
        b = i % 2
        waits = [consts_v] if i == 0 else []
        if i >= 1:
            waits.append((SEM["sV"], vstats[i - 1]))
        if i >= 2:
            waits.append((SEM["sV"], tile_done[i - 2]))
        p.op("sync", lambda s, i=i, b=b: s.dma_start(
            a0_x[b][:], d_x[128 * i:128 * (i + 1), :]), waits=waits, inc="qL", inck=16)
        ldw = p.w("qL")

        p.op("vector", lambda v, b=b: v.reduce_sum(a0_st[:, 1:2], a0_x[b][:], AX), waits=[ldw])
        p.op("vector", lambda v, b=b: v.tensor_tensor(a0_sq[:], a0_x[b][:], a0_x[b][:], ALU.mult))
        vstat = p.op("vector", lambda v: v.reduce_sum(a0_st[:, 0:1], a0_sq[:], AX), inc="sV")
        vstats.append(vstat)

        p.op("scalar", lambda sc: sc.mul(a0_st[:, 2:3], a0_st[:, 1:2], -1.0 / D),
             waits=[(SEM["sV"], vstat)])
        p.op("scalar", lambda sc: sc.activation(a0_st[:, 3:4], a0_st[:, 1:2], AF.Square,
                                                scale=1.0 / D))
        p.op("scalar", lambda sc: sc.mul(a0_st[:, 3:4], a0_st[:, 3:4], -1.0))
        p.op("scalar", lambda sc: sc.mul(a0_st[:, 4:5], a0_st[:, 0:1], 1.0 / D))
        p.op("scalar", lambda sc: sc.add(a0_st[:, 4:5], a0_st[:, 4:5], a0_st[:, 3:4]))
        p.op("scalar", lambda sc: sc.activation(a0_st[:, 5:6], a0_st[:, 4:5], AF.Ln, bias=1e-5))
        crstd = p.op("scalar", lambda sc: sc.activation(a0_st[:, 5:6], a0_st[:, 5:6], AF.Exp,
                                                        scale=-0.5), inc="sC")

        p.op("vector", lambda v, b=b: v.tensor_scalar(
            a0_xn[:], a0_x[b][:], a0_st[:, 2:3], a0_st[:, 5:6], ALU.add, ALU.mult),
            waits=[(SEM["sC"], crstd)], inc="sV")
        vxnb = p.op("vector", lambda v, b=b: v.tensor_copy(a0_xnb[b][:], a0_xn[:]), inc="sV")
        if debug:
            p.op("sync", lambda s_, i=i: s_.dma_start(d_dbgXNT[128 * i:128 * (i + 1), :],
                                                      a0_xn[:]),
                 waits=[(SEM["sV"], vxnb)], inc="qD", inck=16)
            p.op("sync", lambda s_, i=i: s_.dma_start(d_dbgST[:, 8 * i:8 * (i + 1)],
                                                      a0_st[:]),
                 inc="qD", inck=16)

        for c in range(NC):
            tw = [(SEM["sV"], vxnb)] if c == 0 else []
            if i == 0 and c == 0:
                tw.append((SEM["sL"], 1))
            if (i, c - 2) in bevac:
                tw.append((SEM["sC"], bevac[(i, c - 2)]))
            elif (i - 1, c + 6) in bevac:
                tw.append((SEM["sC"], bevac[(i - 1, c + 6)]))
            tp = p.op("tensor", lambda t, b=b, c=c: t.transpose(
                a0_ptb[c % 2][:], a0_xnb[b][:, 128 * c:128 * (c + 1)], eye_b[:]),
                waits=tw, inc="sP")
            bevac[(i, c)] = p.op("scalar", lambda sc, i=i, c=c: sc.copy(
                xn_fm[c][:, 128 * i:128 * (i + 1)], a0_ptb[c % 2][:]),
                waits=[(SEM["sP"], tp)], inc="sC")
        for c in range(NC):
            tw = []
            if (i, c - 2) in fevac:
                tw.append((SEM["sV"], fevac[(i, c - 2)]))
            elif (i - 1, c + 6) in fevac:
                tw.append((SEM["sV"], fevac[(i - 1, c + 6)]))
            tp = p.op("tensor", lambda t, b=b, c=c: t.transpose(
                a0_ptf[c % 2][:], a0_x[b][:, 128 * c:128 * (c + 1)], eye_f[:]),
                waits=tw, inc="sP")
            ew = [(SEM["sP"], tp)]
            if c == 0 and i >= 2:
                ew.append(p.w("qS1"))
            fevac[(i, c)] = p.op("vector", lambda v, b=b, c=c: v.tensor_copy(
                a0_xTt[b][:, 128 * c:128 * (c + 1)], a0_ptf[c % 2][:]),
                waits=ew, inc="sV")
        tile_done.append(p.cnt["sV"])

        p.op("sync", lambda s, i=i, b=b: s.dma_start(
            d_xT[:].rearrange("(c q) s_ -> q c s_", q=128)[:, :, 128 * i:128 * (i + 1)],
            a0_xTt[b][:].rearrange("q (c f) -> q c f", c=NC)),
            waits=[(SEM["sV"], tile_done[i])], inc="qS1", inck=16)

    a0_done = p.snap()
    stackA0.close()

    Qf = stackA.enter_context(nc.sbuf_tensor("a1_Q", [128, S], BF16))
    Kf = stackA.enter_context(nc.sbuf_tensor("a1_K", [128, S], BF16))
    Vfm = stackA.enter_context(nc.sbuf_tensor("a1_Vfm", [128, S], F32))
    Vaug = stackA.enter_context(nc.sbuf_tensor("a1_Vaug", [128, NT * 132], BF16))
    Pt = [stackA.enter_context(nc.sbuf_tensor(f"a1_P{i}", [128, 1024], BF16)) for i in range(2)]
    a1_negm = stackA.enter_context(nc.sbuf_tensor("a1_negm", [128, 2], F32))
    a1_ms = stackA.enter_context(nc.sbuf_tensor("a1_ms", [128, 4], F32))
    a1_mr = stackA.enter_context(nc.sbuf_tensor("a1_mr", [1, 256], F32))
    a1_mt = stackA.enter_context(nc.sbuf_tensor("a1_mt", [1, 8], F32))
    a1_rl = stackA.enter_context(nc.sbuf_tensor("a1_rl", [1, 1024], F32))
    a1_zs = stackA.enter_context(nc.sbuf_tensor("a1_zs", [64, 1024], F32))
    a1_zn = stackA.enter_context(nc.sbuf_tensor("a1_zn", [64, 1024], F32))

    STd = [stackA.enter_context(nc.psum_tensor(f"a1_ST{i}", [128, 1024], F32))
           for i in range(2)]
    ZP = [stackA.enter_context(nc.psum_tensor(f"a1_Z{h}", [65, 1024], F32)) for h in range(2)]

    barA1 = p.barrier()

    p.op("vector", lambda v: v.memset(
        Vaug[:].rearrange("q (c f) -> q c f", c=NT)[:, :, 64:65], 1.0),
        waits=barA1("vector"))
    p.op("vector", lambda v: v.memset(
        Vaug[:].rearrange("q (c f) -> q c f", c=NT)[:, :, 129:130], 1.0))

    genfree = 0
    for g_ in range(8):
        qk_ev = {}
        for which, wsb, dstf in ((0, wq_sb, Qf), (1, wk_sb, Kf)):
            for nt in range(8):
                pb = STd[nt % 2][:, 0:512]
                idx = which * 8 + nt
                tw = []
                if g_ == 0 and idx == 0:
                    tw = barA1("tensor", [w_outer, (SEM["sC"], a0_done["sC"]),
                                          (SEM["sV"], a0_done["sV"])])
                if idx >= 2:
                    tw.append((SEM["sV"], qk_ev[idx - 2]))

                def mmqk(t, wsb=wsb, g_=g_, nt=nt, pb=pb):
                    for c in range(NC):
                        ins = t.matmul(pb,
                                       wsb[:, c * D + 128 * g_:c * D + 128 * (g_ + 1)],
                                       xn_fm[c][:, 512 * nt:512 * (nt + 1)],
                                       start=(c == 0), stop=(c == NC - 1))
                    return ins
                mq = p.op("tensor", mmqk, waits=tw, inc="sP")
                if which == 0:
                    fn = (lambda v, nt=nt, pb=pb, dd=dstf:
                          v.tensor_scalar_mul(dd[:, 512 * nt:512 * (nt + 1)], pb, 0.125))
                else:
                    fn = (lambda v, nt=nt, pb=pb, dd=dstf:
                          v.tensor_copy(dd[:, 512 * nt:512 * (nt + 1)], pb))
                ew = barA1("vector") if (g_ == 0 and idx == 0) else []
                qk_ev[idx] = p.op("vector", fn, waits=ew + [(SEM["sP"], mq)], inc="sV")
        v_ev = {}
        for nt in range(8):
            pb = STd[nt % 2][:, 0:512]
            tw = [(SEM["sV"], qk_ev[14 + nt % 2])] if nt < 2 else [(SEM["sV"], v_ev[nt - 2])]

            def mmv(t, g_=g_, nt=nt, pb=pb):
                for c in range(NC):
                    ins = t.matmul(pb,
                                   wv_sb[:, c * D + 128 * g_:c * D + 128 * (g_ + 1)],
                                   xn_fm[c][:, 512 * nt:512 * (nt + 1)],
                                   start=(c == 0), stop=(c == NC - 1))
                return ins
            mv = p.op("tensor", mmv, waits=tw, inc="sP")
            v_ev[nt] = p.op("vector", lambda v, nt=nt, pb=pb: v.tensor_copy(
                Vfm[:, 512 * nt:512 * (nt + 1)], pb),
                waits=[(SEM["sP"], mv)], inc="sV")
        va_ev = {}
        for tt in range(NT):
            pb = STd[tt % 2][:, 512:640]
            tw = [(SEM["sV"], v_ev[6 + tt % 2])] if tt < 2 else [(SEM["sV"], va_ev[tt - 2])]
            tp = p.op("tensor", lambda t, tt=tt, pb=pb: t.transpose(
                pb, Vfm[:, 128 * tt:128 * (tt + 1)], eye_f[:]),
                waits=tw, inc="sP")
            va_ev[tt] = p.op("vector", lambda v, tt=tt, pb=pb: v.tensor_copy(
                Vaug[:, 132 * tt:132 * tt + 130]
                    .rearrange("q (s f) -> q s f", f=65)[:, :, 0:64],
                pb.rearrange("q (s f) -> q s f", s=2)),
                waits=[(SEM["sP"], tp)], inc="sV")

        p.op("vector", lambda v: v.tensor_reduce(a1_ms[:, 0:1], Qf[:], AX, ALU.max,
                                                 apply_absolute_value=True))
        vk = p.op("vector", lambda v: v.tensor_reduce(a1_ms[:, 1:2], Kf[:], AX, ALU.max,
                                                      apply_absolute_value=True), inc="sV")
        p.op("tensor", lambda t: t.transpose(STd[0][0:1, 0:128], a1_ms[:, 0:1], eye_f[:]),
             waits=[(SEM["sV"], vk)])
        tmr = p.op("tensor", lambda t: t.transpose(STd[0][0:1, 128:256], a1_ms[:, 1:2],
                                                   eye_f[:]), inc="sP")
        p.op("vector", lambda v: v.tensor_copy(a1_mr[0:1, :], STd[0][0:1, 0:256]),
             waits=[(SEM["sP"], tmr)])
        p.op("vector", lambda v: v.tensor_copy(settle_d[:], settle_s[:]))
        p.op("vector", lambda v: v.reduce_max(a1_mt[0:1, 0:1], a1_mr[0:1, 0:128], AX))
        p.op("vector", lambda v: v.reduce_max(a1_mt[0:1, 1:2], a1_mr[0:1, 128:256], AX))
        p.op("vector", lambda v: v.tensor_copy(settle_d[:], settle_s[:]))
        p.op("vector", lambda v: v.tensor_tensor(a1_mt[0:1, 4:5], a1_mt[0:1, 0:1],
                                                 a1_mt[0:1, 1:2], ALU.mult))
        p.op("vector", lambda v: v.tensor_copy(settle_d[:], settle_s[:]))
        vmt = p.op("vector", lambda v: v.tensor_scalar_mul(a1_mt[0:1, 6:7], a1_mt[0:1, 4:5],
                                                           -64.0), inc="sV")
        tbc = p.op("tensor", lambda t: t.matmul(STd[0][0:128, 128:129], ones_r[0:1, :],
                                                a1_mt[0:1, 6:7], start=True, stop=True),
                   waits=[(SEM["sV"], vmt)], inc="sP")
        vnm = p.op("vector", lambda v: v.tensor_copy(a1_negm[:, 0:1], STd[0][0:128, 128:129]),
                   waits=[(SEM["sP"], tbc)], inc="sV")

        if debug and g_ == 0:
            for c in range(NC):
                p.op("sync", lambda s_, c=c: s_.dma_start(
                    d_dbgXN[:, S * c:S * (c + 1)], xn_fm[c][:]),
                    waits=[(SEM["sV"], vnm)], inc="qD", inck=16)
            p.op("sync", lambda s_: s_.dma_start(d_dbgWQ[:], wq_sb[:]), inc="qD", inck=16)
            p.op("sync", lambda s_: s_.dma_start(d_dbgQ[:], Qf[:]),
                 waits=barA1("sync") + [(SEM["sV"], vnm)], inc="qD", inck=16)
            p.op("sync", lambda s_: s_.dma_start(d_dbgK[:], Kf[:]), inc="qD", inck=16)
            p.op("sync", lambda s_: s_.dma_start(d_dbgV[:], Vaug[:]), inc="qD", inck=16)
            p.op("sync", lambda s_: s_.dma_start(d_dbgM[:], a1_negm[:]),
                 inc="qD", inck=16)
        for qh in range(8):
            zh = qh % 2
            expc = {}
            stc = {}
            pvc = {}
            for c in range(34):
                if c < 32:
                    par = c % 2
                    tw = []
                    if qh == 0 and c == 0:
                        tw.append((SEM["sV"], vnm))
                    if c >= 2:
                        tw.append((SEM["sC"], expc[c - 2]))

                    def mmst(t, qh=qh, c=c, par=par):
                        for h in range(2):
                            ins = t.matmul(
                                STd[par][:, 512 * h:512 * (h + 1)],
                                Kf[64 * h:64 * (h + 1), 128 * c:128 * (c + 1)],
                                Qf[64 * h:64 * (h + 1), 512 * qh:512 * (qh + 1)],
                                start=True, stop=True)
                        return ins
                    stc[c] = p.op("tensor", mmst, waits=tw, inc="sP")
                if c >= 2:
                    cc = c - 2

                    def mmpv(t, cc=cc, zh=zh):
                        for h in range(2):
                            ins = t.matmul(
                                ZP[h][0:65, 512 * zh:512 * (zh + 1)],
                                Vaug[:, 132 * cc + 65 * h:132 * cc + 65 * h + 65],
                                Pt[cc % 2][:, 512 * h:512 * (h + 1)],
                                start=(cc == 0), stop=(cc == 31))
                        return ins
                    pvc[cc] = p.op("tensor", mmpv, waits=[(SEM["sC"], expc[cc])], inc="sP")
                if c < 32:
                    ew = barA1("scalar") if (g_ == 0 and qh == 0 and c == 0) else []
                    pw = (SEM["sP"], pvc[c - 2]) if c >= 2 else (SEM["sP"], stc[c])
                    expc[c] = p.op("scalar", lambda sc, c=c: sc.activation(
                        Pt[c % 2][:], STd[c % 2][:], AF.Exp, bias=a1_negm[:, 0:1]),
                        waits=ew + [pw, (SEM["sV"], vnm)], inc="sC")

            if zh == 1:
                qb = qh // 2
                zdone = []
                trb_prev = None
                for h in range(2):
                    rw = [(SEM["sP"], pvc[31])]
                    if trb_prev is not None:
                        rw.append((SEM["sP"], trb_prev))
                    p.op("vector", lambda v, h=h: v.reciprocal(a1_rl[:], ZP[h][64:65, :]),
                         waits=rw)
                    vz = p.op("vector", lambda v, h=h: v.tensor_copy(a1_zs[:], ZP[h][0:64, :]),
                              inc="sV")

                    def mrb(t, h=h):
                        for n in range(2):
                            ins = t.matmul(STd[n][0:64, 512 * h:512 * (h + 1)],
                                           ones_r[0:1, 0:64],
                                           a1_rl[0:1, 512 * n:512 * (n + 1)],
                                           start=True, stop=True)
                        return ins
                    trb = p.op("tensor", mrb, waits=[(SEM["sV"], vz)], inc="sP")
                    trb_prev = trb
                    zw = [(SEM["sP"], trb), p.w("qZ")]

                    def vznf(v, h=h):
                        for n in range(2):
                            ins = v.tensor_tensor(
                                a1_zn[:, 512 * n:512 * (n + 1)], a1_zs[:, 512 * n:512 * (n + 1)],
                                STd[n][0:64, 512 * h:512 * (h + 1)], ALU.mult)
                        return ins
                    vzn = p.op("vector", vznf, waits=zw, inc="sV")
                    st = p.op("sync", lambda s, g_=g_, h=h, qb=qb: s.dma_start(
                        d_zT[128 * g_ + 64 * h:128 * g_ + 64 * (h + 1),
                             1024 * qb:1024 * (qb + 1)], a1_zn[:]),
                        waits=barA1("sync") + [(SEM["sV"], vzn)], inc="qZ", inck=16)
                    zdone.append((vzn, st))
                guard = p.op("tensor", lambda t: t.matmul(
                    STd[0][0:1, 0:1], ones_r[0:1, 0:1], ones_r[0:1, 0:1],
                    start=True, stop=True),
                    waits=[(SEM["sV"], zdone[1][0])], inc="sP")

    a1_done = p.snap()
    stackA.close()

    stackB0 = ExitStack()
    b_xc = [stackB0.enter_context(nc.sbuf_tensor(f"b0_x{c}", [128, 512], F32))
            for c in range(NC)]
    b_zc = [stackB0.enter_context(nc.sbuf_tensor(f"b0_z{c}", [128, 512], F32))
            for c in range(NC)]
    b_x1 = [stackB0.enter_context(nc.sbuf_tensor(f"b0_x1{c}", [128, 512], F32))
            for c in range(NC)]
    b_sq = [stackB0.enter_context(nc.sbuf_tensor(f"b0_sq{i}", [128, 512], F32))
            for i in range(2)]
    b_y0 = [stackB0.enter_context(nc.sbuf_tensor(f"b0_y0{i}", [128, 512], F32))
            for i in range(2)]
    b_t = stackB0.enter_context(nc.sbuf_tensor("b0_t", [128, 512], F32))
    b_stats = stackB0.enter_context(nc.sbuf_tensor("b0_stats", [1, 4 * 512], F32))
    b_S1 = stackB0.enter_context(nc.psum_tensor("b0_S1", [1, 512], F32))
    b_S2 = stackB0.enter_context(nc.psum_tensor("b0_S2", [1, 512], F32))
    b_MB = stackB0.enter_context(nc.psum_tensor("b0_MB", [128, 512], F32))
    b_RB = stackB0.enter_context(nc.psum_tensor("b0_RB", [128, 512], F32))

    barB0 = p.barrier()
    if debug:
        p.op("sync", lambda s_: s_.dma_start(d_dbgZT2[:], d_zT[:]),
             waits=barB0("sync"), inc="qD", inck=16)
    prev_grp = None
    for G in range(8):
        for c in range(NC):
            tw = barB0("sync") if (G == 0 and c == 0) else []
            if prev_grp is not None and c == 0:
                tw.append((SEM["sV"], prev_grp))
            p.op("sync", lambda s, c=c, G=G: s.dma_start(
                b_xc[c][:], d_xT[128 * c:128 * (c + 1), 512 * G:512 * (G + 1)]),
                waits=tw, inc="qL", inck=16)
            p.op("sync", lambda s, c=c, G=G: s.dma_start(
                b_zc[c][:], d_zT[128 * c:128 * (c + 1), 512 * G:512 * (G + 1)]),
                inc="qL", inck=16)
        ld_wave = p.w("qL")
        adds = []
        for c in range(NC):
            tw = [ld_wave, p.w("qS2")]
            if G == 0 and c == 0:
                tw = barB0("vector") + tw
            adds.append(p.op("vector", lambda v, c=c: v.tensor_tensor(
                b_x1[c][:], b_xc[c][:], b_zc[c][:], ALU.add), waits=tw, inc="sV"))
            p.op("sync", lambda s, c=c, G=G: s.dma_start(
                d_x1T[128 * c:128 * (c + 1), 512 * G:512 * (G + 1)], b_x1[c][:]),
                waits=[(SEM["sV"], adds[-1])], inc="qS2", inck=16)
        m1l = None
        for c in range(NC):
            tw = barB0("tensor") if (G == 0 and c == 0) else []
            m1l = p.op("tensor", lambda t, c=c: t.matmul(
                b_S1[:], ones_cf[:], b_x1[c][:], start=(c == 0), stop=(c == NC - 1)),
                waits=tw + [(SEM["sV"], adds[c])], inc="sP")
        sqd = []
        for c in range(NC):
            tw = [] if c < 2 else [(SEM["sP"], sqd[c - 2])]
            vsq = p.op("vector", lambda v, c=c: v.tensor_tensor(
                b_sq[c % 2][:], b_x1[c][:], b_x1[c][:], ALU.mult), waits=tw, inc="sV")
            sqd.append(p.op("tensor", lambda t, c=c: t.matmul(
                b_S2[:], ones_cf[:], b_sq[c % 2][:], start=(c == 0), stop=(c == NC - 1)),
                waits=[(SEM["sV"], vsq)], inc="sP"))
        p.op("vector", lambda v: v.tensor_scalar_mul(b_stats[0:1, 0:512], b_S1[:],
                                                     -1.0 / D), waits=[(SEM["sP"], sqd[7])])
        p.op("vector", lambda v: v.tensor_scalar_mul(b_stats[0:1, 512:1024], b_S2[:], 1.0 / D))
        p.op("vector", lambda v: v.tensor_copy(settle_d[:], settle_s[:]))
        p.op("vector", lambda v: v.tensor_tensor(b_stats[0:1, 1024:1536], b_stats[0:1, 0:512],
                                                 b_stats[0:1, 0:512], ALU.mult))
        p.op("vector", lambda v: v.tensor_copy(settle_d[:], settle_s[:]))
        vvar = p.op("vector", lambda v: v.tensor_tensor(
            b_stats[0:1, 1024:1536], b_stats[0:1, 512:1024], b_stats[0:1, 1024:1536],
            ALU.subtract), inc="sV")
        tw = barB0("scalar") if G == 0 else []
        p.op("scalar", lambda sc: sc.activation(b_stats[0:1, 1536:2048],
                                                b_stats[0:1, 1024:1536], AF.Ln, bias=1e-5),
             waits=tw + [(SEM["sV"], vvar)])
        crs = p.op("scalar", lambda sc: sc.activation(b_stats[0:1, 1536:2048],
                                                      b_stats[0:1, 1536:2048], AF.Exp,
                                                      scale=-0.5), inc="sC")
        mmb = p.op("tensor", lambda t: t.matmul(b_MB[:], ones_r[:], b_stats[0:1, 0:512],
                                                start=True, stop=True),
                   waits=[(SEM["sC"], crs)], inc="sP")
        mrb2 = p.op("tensor", lambda t: t.matmul(b_RB[:], ones_r[:], b_stats[0:1, 1536:2048],
                                                 start=True, stop=True), inc="sP")
        if debug:
            for c in range(NC):
                p.op("sync", lambda s_, c=c, G=G: s_.dma_start(
                    d_dbgX1L[128 * c:128 * (c + 1), 512 * G:512 * (G + 1)], b_x1[c][:]),
                    waits=[(SEM["sP"], mrb2)], inc="qD", inck=16)
        vy = None
        for c in range(NC):
            nw = [(SEM["sP"], mrb2)]
            if c < 2 or True:
                nw.append(p.w("qS3"))
                nw.append(p.w("qHG"))
            p.op("vector", lambda v, c=c: v.tensor_tensor(
                b_t[:], b_x1[c][:], b_MB[:], ALU.add), waits=nw, inc="sV")
            vy = p.op("vector", lambda v, c=c: v.tensor_tensor(
                b_y0[c % 2][:], b_t[:], b_RB[:], ALU.mult), inc="sV")
            p.op("sync", lambda s, c=c, G=G: s.dma_start(
                d_yT[0][128 * c:128 * (c + 1), 512 * G:512 * (G + 1)], b_y0[c % 2][:]),
                waits=[(SEM["sV"], vy)], inc="qS3", inck=16)
            tw = barB0("gpsimd") if (G == 0 and c == 0) else []
            p.op("gpsimd", lambda g, c=c, G=G: g.dma_start(
                d_hT[0][128 * c:128 * (c + 1), 512 * G:512 * (G + 1)], b_y0[c % 2][:]),
                waits=tw + [(SEM["sV"], vy)], inc="qHG", inck=16)
        prev_grp = vy

    b0_done = p.snap()
    stackB0.close()

    def _build_layer(l):
        yin, hin, yout, hout = d_yT[0], d_hT[0], d_yT[1], d_hT[1]

        stackIA = ExitStack()
        iwq = stackIA.enter_context(nc.sbuf_tensor(f"l{l}_ia_wqkv", [128, NC * 1536], BF16))
        iwo = stackIA.enter_context(nc.sbuf_tensor(f"l{l}_ia_wo", [128, 4 * D], BF16))
        tb = [stackIA.enter_context(nc.sbuf_tensor(f"l{l}_ia_tb{k}", [128, 2048], F32))
              for k in range(4)]
        qsb = stackIA.enter_context(nc.sbuf_tensor(f"l{l}_ia_qsb", [128, 64], F32))
        ksb = stackIA.enter_context(nc.sbuf_tensor(f"l{l}_ia_ksb", [128, 64], F32))
        qsw = stackIA.enter_context(nc.sbuf_tensor(f"l{l}_ia_qsw", [128, 64], F32))
        ksw = stackIA.enter_context(nc.sbuf_tensor(f"l{l}_ia_ksw", [128, 64], F32))
        qrow = stackIA.enter_context(nc.sbuf_tensor(f"l{l}_ia_qrow", [1, 64], F32))
        krow = stackIA.enter_context(nc.sbuf_tensor(f"l{l}_ia_krow", [1, 64], F32))
        ht = [stackIA.enter_context(nc.sbuf_tensor(f"l{l}_ia_h{c}", [128, 512], BF16))
              for c in range(NC)]
        yt = [stackIA.enter_context(nc.sbuf_tensor(f"l{l}_ia_y{c}", [128, 512], F32))
              for c in range(NC)]
        qev = stackIA.enter_context(nc.sbuf_tensor(f"l{l}_ia_qev", [128, 512], F32))
        sqs = stackIA.enter_context(nc.sbuf_tensor(f"l{l}_ia_sqs", [128, 512], F32))
        ssum = stackIA.enter_context(nc.sbuf_tensor(f"l{l}_ia_ssum", [128, 16], F32))
        rot = stackIA.enter_context(nc.sbuf_tensor(f"l{l}_ia_rot", [128, 512], F32))
        t1s = stackIA.enter_context(nc.sbuf_tensor(f"l{l}_ia_t1", [128, 512], F32))
        qpb8 = [stackIA.enter_context(nc.sbuf_tensor(f"l{l}_ia_qpb{i}", [128, 512], BF16))
                for i in range(8)]
        ia_state = {"crot": None, "vqp": None, "ppar": 0, "evq": []}
        tp_last = {}
        qfm = stackIA.enter_context(nc.sbuf_tensor(f"l{l}_ia_qfm", [128, 4 * 512], BF16))
        kfmA = stackIA.enter_context(nc.sbuf_tensor(f"l{l}_ia_kfmA", [128, 4 * 512], BF16))
        kfmB = stackIA.enter_context(nc.sbuf_tensor(f"l{l}_ia_kfmB", [128, 4 * 512], BF16))
        VA = stackIA.enter_context(nc.sbuf_tensor(f"l{l}_ia_VA", [128, 4 * 8 * 66], BF16))
        VB = stackIA.enter_context(nc.sbuf_tensor(f"l{l}_ia_VB", [128, 4 * 8 * 66], BF16))
        Pe = stackIA.enter_context(nc.sbuf_tensor(f"l{l}_ia_Pe", [128, 640], BF16))
        Po = stackIA.enter_context(nc.sbuf_tensor(f"l{l}_ia_Po", [128, 640], BF16))
        PTe = stackIA.enter_context(nc.sbuf_tensor(f"l{l}_ia_PTe", [128, 640], BF16))
        PTo = stackIA.enter_context(nc.sbuf_tensor(f"l{l}_ia_PTo", [128, 640], BF16))
        nme = stackIA.enter_context(nc.sbuf_tensor(f"l{l}_ia_nme", [128, 1], F32))
        nmo = stackIA.enter_context(nc.sbuf_tensor(f"l{l}_ia_nmo", [128, 1], F32))
        irl = stackIA.enter_context(nc.sbuf_tensor(f"l{l}_ia_irl", [1, 512], F32))
        izs = stackIA.enter_context(nc.sbuf_tensor(f"l{l}_ia_izs", [64, 512], F32))
        zfm = stackIA.enter_context(nc.sbuf_tensor(f"l{l}_ia_zfm", [128, 4 * 512], BF16))
        y1e = [stackIA.enter_context(nc.sbuf_tensor(f"l{l}_ia_y1e{i}", [128, 512], F32))
               for i in range(2)]

        PSA = stackIA.enter_context(nc.psum_tensor(f"l{l}_ia_PSA", [128, 1536], F32))
        PSB = stackIA.enter_context(nc.psum_tensor(f"l{l}_ia_PSB", [128, 1536], F32))
        PTAp = stackIA.enter_context(nc.psum_tensor(f"l{l}_ia_PTAp", [128, 640], BF16))
        PTBp = stackIA.enter_context(nc.psum_tensor(f"l{l}_ia_PTBp", [128, 640], BF16))

        barIA = p.barrier()

        p.op("gpsimd", lambda g, l=l: g.dma_start(
            iwq[:].rearrange("q (c f) -> q c f", c=NC),
            d_iwqkv[l][:].rearrange("(c q) f -> q c f", q=128)),
            waits=barIA("gpsimd"), inc="qWG", inck=16)
        p.op("gpsimd", lambda g, l=l: g.dma_start(
            iwo[:].rearrange("q (c f) -> q c f", c=4),
            d_iwo[l][:].rearrange("(c q) f -> q c f", q=128)), inc="qWG", inck=16)
        wld = p.w("qWG")
        for k in range(4):
            p.op("sync", lambda s, k=k: s.dma_start(
                tb[k][:].rearrange("q (c f) -> q c f", f=64),
                d_tab[k][:].rearrange("(c q) f -> q c f", q=128)),
                waits=barIA("sync"), inc="qL", inck=16)
        p.op("sync", lambda s, l=l: s.dma_start(qrow[:], d_qs[l][:]), inc="qL", inck=16)
        p.op("sync", lambda s, l=l: s.dma_start(krow[:], d_ks[l][:]), inc="qL", inck=16)
        tabld = p.w("qL")
        tq = p.op("tensor", lambda t: t.matmul(PSA[0:128, 0:64], ones_r[:], qrow[:],
                                               start=True, stop=True),
                  waits=barIA("tensor") + [tabld], inc="sP")
        tk = p.op("tensor", lambda t: t.matmul(PSB[0:128, 0:64], ones_r[:], krow[:],
                                               start=True, stop=True), inc="sP")
        vq1 = p.op("vector", lambda v: v.tensor_copy(qsb[:], PSA[0:128, 0:64]),
                   waits=barIA("vector") + [(SEM["sP"], tq)], inc="sV")
        vk1 = p.op("vector", lambda v: v.tensor_copy(ksb[:], PSB[0:128, 0:64]),
                   waits=[(SEM["sP"], tk)], inc="sV")
        p.op("vector", lambda v: v.tensor_copy(qsw[:, 0:32], qsb[:, 32:64]))
        p.op("vector", lambda v: v.tensor_copy(qsw[:, 32:64], qsb[:, 0:32]))
        p.op("vector", lambda v: v.tensor_copy(ksw[:, 0:32], ksb[:, 32:64]))
        p.op("vector", lambda v: v.tensor_copy(ksw[:, 32:64], ksb[:, 0:32]))
        for k, scl in ((0, qsb), (1, qsw), (2, ksb), (3, ksw)):
            p.op("vector", lambda v, k=k, scl=scl: v.tensor_tensor(
                tb[k][:].rearrange("q (c f) -> q c f", f=64),
                tb[k][:].rearrange("q (c f) -> q c f", f=64),
                scl[:].unsqueeze(1).broadcast_to([128, 32, 64]), ALU.mult),
                waits=[tabld])
        vfold = p.op("vector", lambda v: v.tensor_copy(qsw[0:1, 0:1], qsw[0:1, 0:1]), inc="sV")

        kprev, kcur = kfmA, kfmB
        Vprev, Vcur = VA, VB
        for vb_ in (VA, VB):
            p.op("vector", lambda v, vb_=vb_: v.memset(
                vb_[:].rearrange("q (c f) -> q c f", f=66)[:, :, 64:65], 1.0))

        ia_done_prev_w = None
        ia_z = {"last_exp": None, "zg": [None, None]}
        for w in range(8):
            kprev, kcur = kcur, kprev
            Vprev, Vcur = Vcur, Vprev
            hmemv = 0
            for c in range(NC):
                tw = []
                if ia_done_prev_w is not None and c == 0:
                    tw = [(SEM["sP"], ia_done_prev_w[0]), (SEM["sV"], ia_done_prev_w[1])]
                if c < 4 or w > 0:
                    off = 512 * w if c < 4 else 512 * w - 1
                    p.op("sync", lambda s, c=c, off=off: s.dma_start(
                        ht[c][:], hin[128 * c:128 * (c + 1), off:off + 512]),
                        waits=tw, inc="qL", inck=16)
                else:
                    hmemv = p.op("vector", lambda v, c=c: v.memset(ht[c][:, 0:1], 0.0),
                                 waits=tw, inc="sV")
                    p.op("sync", lambda s, c=c: s.dma_start(
                        ht[c][:, 1:512], hin[128 * c:128 * (c + 1), 0:511]),
                        waits=tw, inc="qL", inck=16)
            for c in range(NC):
                p.op("sync", lambda s, c=c, w=w: s.dma_start(
                    yt[c][:], yin[128 * c:128 * (c + 1), 512 * w:512 * (w + 1)]),
                    inc="qL", inck=16)
            hldw = p.w("qL")

            qk_psum_free = []
            pmarks = {}

            def emit_gen(sub):
                PQ = PSA if sub % 2 == 0 else PSB
                tw = [hldw, (SEM["sV"], vfold)]
                if w == 0 and sub == 0:
                    tw.append(wld)
                    tw.append((SEM["sV"], hmemv))
                if len(qk_psum_free) >= 2:
                    tw.append((SEM["sV"], qk_psum_free[-2]))

                def mmqkv(t, sub=sub, PQ=PQ):
                    for c in range(NC):
                        for j in range(3):
                            ins = t.matmul(PQ[:, 512 * j:512 * (j + 1)],
                                           ht[c][:, 128 * sub:128 * (sub + 1)],
                                           iwq[:, 1536 * c + 512 * j:1536 * c + 512 * (j + 1)],
                                           start=(c == 0), stop=(c == NC - 1))
                    return ins
                return p.op("tensor", mmqkv, waits=tw, inc="sP"), PQ

            def emit_proc(sub, mql, PQ):
                st_ = ia_state
                p.op("vector", lambda v, sub=sub, PQ=PQ, Vcur=Vcur: v.tensor_copy(
                    Vcur[:].rearrange("q (cc hh f) -> q cc hh f", cc=4, hh=8)[:, sub, :, 0:64],
                    PQ[:, 1024:1536].rearrange("q (hh f) -> q hh f", hh=8)),
                    waits=[(SEM["sP"], mql)])
                pmarks[sub] = []
                vev = None
                for iq, (dst, t_cs, t_ss, bias) in enumerate(
                        ((qfm, tb[0], tb[1], LN8), (kcur, tb[2], tb[3], 0.0))):
                    off = 512 * iq
                    buf = qpb8[2 * sub + iq]
                    ew = [(SEM["sP"], mql)]
                    if st_["crot"] is not None:
                        ew.append((SEM["sC"], st_["crot"]))
                    vev = p.op("vector", lambda v, PQ=PQ, off=off: v.tensor_copy(
                        qev[:], PQ[:, off:off + 512]), waits=ew, inc="sV")
                    cw = barIA("scalar") if (w == 0 and sub == 0 and iq == 0) else []
                    csq = p.op("scalar", lambda sc: sc.activation(sqs[:], qev[:], AF.Square),
                               waits=cw + [(SEM["sV"], vev)], inc="sC")
                    vs2 = p.op("vector", lambda v: v.tensor_reduce(
                        ssum[:, 0:8], sqs[:].rearrange("q (hh f) -> q hh f", hh=8), AX,
                        ALU.add), waits=[(SEM["sC"], csq)], inc="sV")
                    p.op("scalar", lambda sc: sc.activation(ssum[:, 8:16], ssum[:, 0:8],
                                                            AF.Ln, bias=1e-30),
                         waits=[(SEM["sV"], vs2)])
                    crs2 = p.op("scalar", lambda sc, bias=bias: sc.activation(
                        ssum[:, 0:8], ssum[:, 8:16], AF.Exp, scale=-0.5, bias=bias),
                        inc="sC")
                    rw = [(SEM["sC"], crs2)]
                    if st_["vqp"] is not None:
                        rw.append((SEM["sV"], st_["vqp"]))
                    p.op("scalar", lambda sc: sc.mul(
                        rot[:].rearrange("q (hh f) -> q hh f", hh=8)[:, :, 0:32],
                        qev[:].rearrange("q (hh f) -> q hh f", hh=8)[:, :, 32:64], -1.0),
                        waits=rw)
                    st_["crot"] = crot = p.op("scalar", lambda sc: sc.copy(
                        rot[:].rearrange("q (hh f) -> q hh f", hh=8)[:, :, 32:64],
                        qev[:].rearrange("q (hh f) -> q hh f", hh=8)[:, :, 0:32]), inc="sC")
                    t1w = []
                    if st_["vqp"] is not None:
                        t1w.append((SEM["sV"], st_["vqp"]))
                    p.op("vector", lambda v, t_cs=t_cs, w=w, sub=sub: v.tensor_tensor(
                        t1s[:].rearrange("q (hh f) -> q hh f", hh=8),
                        qev[:].rearrange("q (hh f) -> q hh f", hh=8),
                        t_cs[:, 64 * (4 * w + sub):64 * (4 * w + sub) + 64]
                            .unsqueeze(1).broadcast_to([128, 8, 64]), ALU.mult), waits=t1w)
                    p.op("vector", lambda v, t_ss=t_ss, w=w, sub=sub: v.tensor_tensor(
                        rot[:].rearrange("q (hh f) -> q hh f", hh=8),
                        rot[:].rearrange("q (hh f) -> q hh f", hh=8),
                        t_ss[:, 64 * (4 * w + sub):64 * (4 * w + sub) + 64]
                            .unsqueeze(1).broadcast_to([128, 8, 64]), ALU.mult),
                        waits=[(SEM["sC"], crot)])
                    p.op("vector", lambda v: v.tensor_tensor(t1s[:], t1s[:], rot[:], ALU.add))
                    qw = [(SEM["sC"], crs2)]
                    if tp_last.get(2 * sub + iq) is not None:
                        qw.append((SEM["sP"], tp_last[2 * sub + iq]))
                    vqp = p.op("vector", lambda v, buf=buf: v.tensor_tensor(
                        buf[:].rearrange("q (hh f) -> q hh f", hh=8),
                        t1s[:].rearrange("q (hh f) -> q hh f", hh=8),
                        ssum[:, 0:8].unsqueeze(2).broadcast_to([128, 8, 64]), ALU.mult),
                        waits=qw, inc="sV")
                    st_["vqp"] = vqp
                    pmarks[sub].append((vqp, dst, buf, iq))
                qk_psum_free.append(vev)

            def emit_T(sub):
                st_ = ia_state
                for vqp, dst, buf, iq in pmarks[sub]:
                    for pb_ in range(4):
                        pt = PTAp if st_["ppar"] == 0 else PTBp
                        st_["ppar"] ^= 1
                        tw2 = [(SEM["sV"], vqp)]
                        if st_["evq"]:
                            tw2.append((SEM["sC"], st_["evq"][0]))
                        tp = p.op("tensor", lambda t, pb_=pb_, pt=pt, buf=buf: t.transpose(
                            pt[:, 0:128], buf[:, 128 * pb_:128 * (pb_ + 1)], eye_b[:]),
                            waits=tw2, inc="sP")
                        ev = p.op("scalar", lambda sc, dst=dst, pb_=pb_, sub=sub,
                                  pt=pt: sc.copy(
                            dst[:, 512 * pb_ + 128 * sub:512 * pb_ + 128 * (sub + 1)],
                            pt[:, 0:128]), waits=[(SEM["sP"], tp)], inc="sC")
                        st_["evq"].append(ev)
                        if len(st_["evq"]) > 2:
                            st_["evq"].pop(0)
                        tp_last[2 * sub + iq] = tp

            m0, q0 = emit_gen(0)
            emit_proc(0, m0, q0)
            m1_, q1 = emit_gen(1)
            emit_proc(1, m1_, q1)
            emit_T(0)
            m2, q2 = emit_gen(2)
            emit_proc(2, m2, q2)
            emit_T(1)
            m3, q3 = emit_gen(3)
            emit_proc(3, m3, q3)
            emit_T(2)
            emit_T(3)
            qk_done = [qk_psum_free[-1]]
            win_qk = p.cnt["sC"]

            zpair = []
            ZAe = PSA[0:65, 1024:1536]
            ZAo = PSB[0:65, 1024:1536]

            def emit_zdiv(pv_pair, pr_):
                vlast = None
                trb_pv = None
                for e in range(2):
                    ZA_ = ZAe if e == 0 else ZAo
                    ZRB_ = PSA[0:64, 1024:1536] if e == 0 else PSB[0:64, 1024:1536]
                    rw = [(SEM["sP"], pv_pair[e])]
                    if trb_pv is not None:
                        rw.append((SEM["sP"], trb_pv))
                    p.op("vector", lambda v, ZA_=ZA_: v.reciprocal(irl[:], ZA_[64:65, :]),
                         waits=rw)
                    vz2 = p.op("vector", lambda v, ZA_=ZA_: v.tensor_copy(
                        izs[:], ZA_[0:64, :]), inc="sV")
                    trb2 = p.op("tensor", lambda t, ZRB_=ZRB_: t.matmul(
                        ZRB_, ones_r[0:1, 0:64], irl[:], start=True, stop=True),
                        waits=[(SEM["sV"], vz2)], inc="sP")
                    trb_pv = trb2
                    vlast = p.op("vector", lambda v, e=e, pr_=pr_, ZRB_=ZRB_: v.tensor_tensor(
                        zfm[64 * e:64 * (e + 1), 512 * pr_:512 * (pr_ + 1)], izs[:],
                        ZRB_, ALU.mult),
                        waits=[(SEM["sP"], trb2)], inc="sV")
                    ia_z["zg"][e] = vlast
                zpair.append(vlast)

            pvdone = []
            for pr in range(4):
                for sub in range(4):
                    if w > 0:
                        n1 = 512 - 128 * sub
                        n2 = 128 * (sub + 1)
                    else:
                        n1 = 0
                        n2 = 128 * (sub + 1)
                    N = n1 + n2
                    nch = N // 128
                    scw = win_qk if ia_z["last_exp"] is None else \
                        max(win_qk, ia_z["last_exp"])
                    stw = [(SEM["sC"], scw),
                           (SEM["sV"], qk_psum_free[-1])]

                    def mmS(t, pr=pr, sub=sub, n1=n1, n2=n2, w=w,
                            kprev=kprev, kcur=kcur):
                        N_ = n1 + n2
                        for e in range(2):
                            PS_ = PSA if e == 0 else PSB
                            lhs = qfm[64 * e:64 * (e + 1),
                                      512 * pr + 128 * sub:512 * pr + 128 * (sub + 1)]
                            if n1 > 0:
                                t.matmul(PS_[:, 0:n1], lhs,
                                         kprev[64 * e:64 * (e + 1),
                                               512 * pr + 512 - n1:512 * pr + 512],
                                         start=True, stop=False)
                                if n1 < 512:
                                    t.matmul(PS_[:, n1:512], lhs,
                                             kcur[64 * e:64 * (e + 1),
                                                  512 * pr:512 * pr + 512 - n1],
                                             start=False, stop=False)
                                t.matmul(PS_[:, 512:N_], lhs,
                                         kcur[64 * e:64 * (e + 1),
                                              512 * pr + 512 - n1:512 * pr + n2],
                                         start=True, stop=False)
                                t.matmul(PS_[:, 0:128], mk_lt[:], negeye[:],
                                         start=False, stop=False)
                            else:
                                t.matmul(PS_[:, 0:N_], lhs,
                                         kcur[64 * e:64 * (e + 1), 512 * pr:512 * pr + n2],
                                         start=True, stop=False)
                            ins = t.matmul(PS_[:, N_ - 128:N_], mk_gt[:], negeye[:],
                                           start=False, stop=True)
                        return ins
                    ms = p.op("tensor", mmS, waits=stw, inc="sP")
                    if ia_z.get("pend") is not None:
                        ia_z["pend"]()
                        if ia_z["pend_tag"] == 3:
                            ia_z["pv2"] = (pvdone[-2], pvdone[-1])
                        ia_z["pend"] = None
                        if sub == 0 and pr > 0:
                            emit_zdiv(ia_z["pv2"], pr - 1)
                    ces = []
                    for e in range(2):
                        PS_ = PSA if e == 0 else PSB
                        nm_ = nme if e == 0 else nmo
                        P_ = Pe if e == 0 else Po
                        vnm2 = p.op("vector", lambda v, PS_=PS_, nm_=nm_, N=N: v.tensor_reduce(
                            nm_[:], PS_[:, 0:N], AX, ALU.max, negate=True),
                            waits=[(SEM["sP"], ms)], inc="sV")
                        cw2 = [(SEM["sV"], vnm2)]
                        if debug:
                            cw2.append(p.w("qD"))
                        ce = p.op("scalar", lambda sc, PS_=PS_, nm_=nm_, P_=P_, N=N:
                                  sc.activation(P_[:, 0:N], PS_[:, 0:N], AF.Exp,
                                                bias=nm_[:]),
                                  waits=cw2, inc="sC")
                        ces.append(ce)
                        if debug and l == 0 and w == 1 and pr == 0 and sub == 1 and e == 0:
                            p.op("sync", lambda s_, N=N: s_.dma_start(
                                d_dbgPP[:, 0:N], Pe[:, 0:N]),
                                waits=[(SEM["sC"], ce)], inc="qD", inck=16)

                            p.op("vector", lambda v, PS_=PS_, nm_=nm_: v.tensor_copy(
                                sqs[:, 0:1], nm_[:]), waits=[(SEM["sC"], ce)])
                            p.op("vector", lambda v, PS_=PS_: v.tensor_copy(
                                sqs[:], PS_[:, 0:512]), waits=[(SEM["sC"], ce)])
                            vds = p.op("vector", lambda v: v.tensor_copy(
                                sqs[0:1, 0:1], sqs[0:1, 0:1]), inc="sV")
                            p.op("gpsimd", lambda g_: g_.dma_start(
                                d_dbgKI[:, 2560:3072], sqs[:]),
                                waits=[(SEM["sV"], vds)], inc="qD", inck=16)
                    ia_z["last_exp"] = ces[-1]
                    evs = []
                    for e in range(2):
                        P_ = Pe if e == 0 else Po
                        pt = PTAp if e == 0 else PTBp
                        PT_ = PTe if e == 0 else PTo
                        tp = None
                        for j in range(nch):
                            tw2 = [(SEM["sC"], ces[e])] if j == 0 else []
                            tp = p.op("tensor", lambda t, P_=P_, j=j, pt=pt: t.transpose(
                                pt[:, 128 * j:128 * (j + 1)], P_[:, 128 * j:128 * (j + 1)],
                                eye_b[:]), waits=tw2, inc="sP")
                        evs.append(p.op("scalar", lambda sc, PT_=PT_, pt=pt, nch=nch:
                                        sc.copy(PT_[:, 0:128 * nch], pt[:, 0:128 * nch]),
                                        waits=[(SEM["sP"], tp)], inc="sC"))
                    def mk_pv(evs=evs, pr=pr, sub=sub, nch=nch, w=w,
                              Vprev=Vprev, Vcur=Vcur):
                        def go():
                            for e in range(2):
                                PT_ = PTe if e == 0 else PTo
                                ZA_ = ZAe if e == 0 else ZAo

                                def mmPV(t, e=e, pr=pr, sub=sub, nch=nch, w=w,
                                         PT_=PT_, ZA_=ZA_, Vprev=Vprev, Vcur=Vcur):
                                    for j in range(nch):
                                        if w > 0:
                                            comb = 128 * sub + 128 * j
                                            if comb < 512:
                                                vsrc = Vprev
                                                chunk = comb // 128
                                            else:
                                                vsrc = Vcur
                                                chunk = (comb - 512) // 128
                                        else:
                                            vsrc = Vcur
                                            chunk = j
                                        hh = 2 * pr + e
                                        ins = t.matmul(
                                            ZA_[0:65, 128 * sub:128 * (sub + 1)],
                                            vsrc[:, 66 * (8 * chunk + hh):
                                                 66 * (8 * chunk + hh) + 65],
                                            PT_[:, 128 * j:128 * (j + 1)],
                                            start=(j == 0), stop=(j == nch - 1))
                                    return ins
                                pvw = [(SEM["sC"], evs[e])]
                                if sub == 0 and ia_z["zg"][e] is not None:
                                    pvw.append((SEM["sV"], ia_z["zg"][e]))
                                pvdone.append(p.op("tensor", mmPV, waits=pvw,
                                                   inc="sP"))
                        return go
                    ia_z["pend"] = mk_pv()
                    ia_z["pend_tag"] = sub
            ia_z["pend"]()
            ia_z["pend"] = None
            ia_z["pv2"] = (pvdone[-2], pvdone[-1])
            emit_zdiv(ia_z["pv2"], 3)

            if debug and l == 0 and w == 1:
                p.op("sync", lambda s_: s_.dma_start(d_dbgQF[:], qfm[:]),
                     waits=[(SEM["sV"], zpair[-1])], inc="qD", inck=16)
                p.op("sync", lambda s_, kcur=kcur: s_.dma_start(d_dbgKF[:], kcur[:]),
                     inc="qD", inck=16)
                p.op("sync", lambda s_, kprev=kprev: s_.dma_start(d_dbgKP[:], kprev[:]),
                     inc="qD", inck=16)
                p.op("sync", lambda s_, Vcur=Vcur: s_.dma_start(d_dbgVC[:], Vcur[:]),
                     inc="qD", inck=16)
                p.op("sync", lambda s_: s_.dma_start(d_dbgZF[:], zfm[:]), inc="qD", inck=16)
            wo_done = []
            for dc in range(NC):
                WO = PSA if dc % 2 == 0 else PSB
                tw = [(SEM["sV"], zpair[-1])]
                if len(wo_done) >= 2:
                    tw.append((SEM["sV"], wo_done[-2][0]))

                def mmwo(t, dc=dc, WO=WO):
                    for ic in range(4):
                        ins = t.matmul(WO[:, 0:512],
                                       iwo[:, 1024 * ic + 128 * dc:1024 * ic + 128 * (dc + 1)],
                                       zfm[:, 512 * ic:512 * (ic + 1)],
                                       start=(ic == 0), stop=(ic == 3))
                    return ins
                mo = p.op("tensor", mmwo, waits=tw, inc="sP")
                vw = [(SEM["sP"], mo), hldw, p.w("qS3"), p.w("qHG")]
                vy1 = p.op("vector", lambda v, dc=dc, WO=WO: v.tensor_tensor(
                    y1e[dc % 2][:], WO[:, 0:512], yt[dc][:], ALU.add),
                    waits=vw, inc="sV")
                p.op("sync", lambda s, dc=dc, w=w: s.dma_start(
                    yout[128 * dc:128 * (dc + 1), 512 * w:512 * (w + 1)], y1e[dc % 2][:]),
                    waits=[(SEM["sV"], vy1)], inc="qS3", inck=16)
                p.op("gpsimd", lambda g, dc=dc, w=w: g.dma_start(
                    hout[128 * dc:128 * (dc + 1), 512 * w:512 * (w + 1)], y1e[dc % 2][:]),
                    waits=[(SEM["sV"], vy1)], inc="qHG", inck=16)
                wo_done.append((vy1, mo))
            ia_done_prev_w = (wo_done[-1][1], wo_done[-1][0])

        stackIA.close()

        if debug:
            p.op("sync", lambda s_, l=l: s_.dma_start(d_dbgY[2 * l][:], yout[:]),
                 waits=[p.w("qS3"), p.w("qHG")], inc="qD", inck=16)
        stackF = ExitStack()
        w1s = stackF.enter_context(nc.sbuf_tensor(f"l{l}_f_w1", [128, NC * FF], BF16))
        w2s = stackF.enter_context(nc.sbuf_tensor(f"l{l}_f_w2", [128, NHC * D], BF16))
        gsb = stackF.enter_context(nc.sbuf_tensor(f"l{l}_f_g", [128, 8], F32))
        bsb = stackF.enter_context(nc.sbuf_tensor(f"l{l}_f_b", [128, 8], F32))
        b1as = stackF.enter_context(nc.sbuf_tensor(f"l{l}_f_b1a", [128, 22], F32))
        b1gs = stackF.enter_context(nc.sbuf_tensor(f"l{l}_f_b1g", [128, 22], F32))
        b2s = stackF.enter_context(nc.sbuf_tensor(f"l{l}_f_b2", [128, 8], F32))
        fht = [stackF.enter_context(nc.sbuf_tensor(f"l{l}_f_h{c}", [128, 512], BF16))
               for c in range(NC)]
        fyt = [stackF.enter_context(nc.sbuf_tensor(f"l{l}_f_y{c}", [128, 512], F32))
               for c in range(2)]
        fhh = [stackF.enter_context(nc.sbuf_tensor(f"l{l}_f_hh{i}", [128, 512], BF16))
               for i in range(2)]
        fhn2 = [[stackF.enter_context(nc.sbuf_tensor(f"l{l}_f_hn{gb}_{c}", [128, 512], BF16))
                 for c in range(NC)] for gb in range(2)]
        fstat = stackF.enter_context(nc.sbuf_tensor(f"l{l}_f_stat", [1, 2048], F32))
        fts = stackF.enter_context(nc.sbuf_tensor(f"l{l}_f_ts", [128, 512], F32))
        fga = [stackF.enter_context(nc.sbuf_tensor(f"l{l}_f_ga{i}", [128, 512], F32))
               for i in range(2)]
        fav = [stackF.enter_context(nc.sbuf_tensor(f"l{l}_f_av{i}", [128, 512], F32))
               for i in range(2)]
        fm = stackF.enter_context(nc.sbuf_tensor(f"l{l}_f_m", [128, NHC * 512], BF16))
        fy2 = [stackF.enter_context(nc.sbuf_tensor(f"l{l}_f_y2{i}", [128, 512], F32))
               for i in range(2)]
        fS1 = stackF.enter_context(nc.psum_tensor(f"l{l}_f_S1", [1, 512], F32))
        fS2 = stackF.enter_context(nc.psum_tensor(f"l{l}_f_S2", [1, 512], F32))
        fMB = stackF.enter_context(nc.psum_tensor(f"l{l}_f_MB", [128, 512], F32))
        fRB = stackF.enter_context(nc.psum_tensor(f"l{l}_f_RB", [128, 512], F32))
        fA = [stackF.enter_context(nc.psum_tensor(f"l{l}_f_A{i}", [128, 512], F32))
              for i in range(2)]
        fG = [stackF.enter_context(nc.psum_tensor(f"l{l}_f_G{i}", [128, 512], F32))
              for i in range(2)]

        barF = p.barrier()
        p.op("gpsimd", lambda g, l=l: g.dma_start(
            w1s[:].rearrange("q (c f) -> q c f", c=NC),
            d_iw1[l][:].rearrange("(c q) f -> q c f", q=128)),
            waits=barF("gpsimd"), inc="qWG", inck=16)
        p.op("gpsimd", lambda g, l=l: g.dma_start(
            w2s[:].rearrange("q (c f) -> q c f", c=NHC),
            d_iw2[l][:].rearrange("(c q) f -> q c f", q=128)), inc="qWG", inck=16)
        fwld = p.w("qWG")
        for dst, src in ((gsb, d_g[l]), (bsb, d_b[l]), (b1as, d_b1a[l]),
                         (b1gs, d_b1g[l]), (b2s, d_b2[l])):
            p.op("sync", lambda s, dst=dst, src=src: s.dma_start(dst[:], src[:]),
                 waits=barF("sync"), inc="qL", inck=16)
        fsld = p.w("qL")
        p.op("vector", lambda v: v.memset(fm[:, 512 * 21:512 * 22], 0.0),
             waits=barF("vector"))

        prev_hn = None
        prev_w2 = None
        prev_w2_pair = None
        gstates = {}

        def f_loads(G):
            fhn = fhn2[G % 2]
            fmemv = 0
            for c in range(NC):
                tw = [] if (prev_hn is None or c > 0) else [(SEM["sV"], prev_hn)]
                if c < 4 or G > 0:
                    off = 512 * G if c < 4 else 512 * G - 1
                    p.op("sync", lambda s, c=c, off=off, hT_=hout: s.dma_start(
                        fht[c][:], hT_[128 * c:128 * (c + 1), off:off + 512]),
                        waits=tw, inc="qL", inck=16)
                else:
                    fmemv = p.op("vector", lambda v, c=c: v.memset(fht[c][:, 0:1], 0.0),
                                 waits=tw, inc="sV")
                    p.op("sync", lambda s, c=c, hT_=hout: s.dma_start(
                        fht[c][:, 1:512], hT_[128 * c:128 * (c + 1), 0:511]),
                        waits=tw, inc="qL", inck=16)
            gstates[G] = {"fhn": fhn, "fmemv": fmemv, "fld": p.w("qL")}

        def f_stats(G):
            st_ = gstates[G]
            for c in range(NC):
                tw = barF("tensor") + [fwld] if (G == 0 and c == 0) else []
                if c == 0:
                    tw.append(st_["fld"])
                if G == 0 and c == 0:
                    tw.append((SEM["sV"], st_["fmemv"]))
                p.op("tensor", lambda t, c=c: t.matmul(
                    fS1[:], ones_cb[:], fht[c][:], start=(c == 0), stop=(c == NC - 1)),
                    waits=tw, inc="sP")
            sq2 = []
            for c in range(NC):
                tw = [st_["fld"]] if c == 0 else []
                if c >= 2:
                    tw.append((SEM["sP"], sq2[c - 2]))
                vb = p.op("vector", lambda v, c=c: v.tensor_tensor(
                    fhh[c % 2][:], fht[c][:], fht[c][:], ALU.mult), waits=tw, inc="sV")
                sq2.append(p.op("tensor", lambda t, c=c: t.matmul(
                    fS2[:], ones_cb[:], fhh[c % 2][:], start=(c == 0), stop=(c == NC - 1)),
                    waits=[(SEM["sV"], vb)], inc="sP"))
            p.op("vector", lambda v: v.tensor_scalar_mul(fstat[0:1, 0:512], fS1[:], -1.0 / D),
                 waits=[(SEM["sP"], sq2[7])])
            p.op("vector", lambda v: v.tensor_scalar_mul(fstat[0:1, 512:1024], fS2[:],
                                                         1.0 / D))
            p.op("vector", lambda v: v.tensor_copy(settle_d[:], settle_s[:]))
            p.op("vector", lambda v: v.tensor_tensor(
                fstat[0:1, 1024:1536], fstat[0:1, 0:512], fstat[0:1, 0:512], ALU.mult))
            p.op("vector", lambda v: v.tensor_copy(settle_d[:], settle_s[:]))
            vv2 = p.op("vector", lambda v: v.tensor_tensor(
                fstat[0:1, 1024:1536], fstat[0:1, 512:1024], fstat[0:1, 1024:1536],
                ALU.subtract), inc="sV")
            cw = barF("scalar") if G == 0 else []
            p.op("scalar", lambda sc: sc.activation(fstat[0:1, 1536:2048],
                                                    fstat[0:1, 1024:1536], AF.Ln, bias=1e-5),
                 waits=cw + [(SEM["sV"], vv2)])
            st_["crs3"] = p.op("scalar", lambda sc: sc.activation(
                fstat[0:1, 1536:2048], fstat[0:1, 1536:2048], AF.Exp, scale=-0.5), inc="sC")

        def f_bcast_hn(G):
            nonlocal prev_hn
            st_ = gstates[G]
            fhn = st_["fhn"]
            p.op("tensor", lambda t: t.matmul(fMB[:], ones_r[:], fstat[0:1, 0:512],
                                              start=True, stop=True),
                 waits=[(SEM["sC"], st_["crs3"])], inc="sP")
            mb2 = p.op("tensor", lambda t: t.matmul(fRB[:], ones_r[:], fstat[0:1, 1536:2048],
                                                    start=True, stop=True), inc="sP")
            hn_done = []
            for c in range(NC):
                nw2 = [(SEM["sP"], mb2)]
                if c == 0 and G >= 1:
                    nw2.append((SEM["sP"], gstates[G - 1]["w1m0"]))
                p.op("vector", lambda v, c=c: v.tensor_tensor(
                    fts[:], fht[c][:], fMB[:], ALU.add), waits=nw2)
                p.op("vector", lambda v: v.tensor_tensor(fts[:], fts[:], fRB[:], ALU.mult))
                hn_done.append(p.op("vector", lambda v, c=c, fhn=fhn: v.tensor_scalar(
                    fhn[c][:], fts[:], gsb[:, c:c + 1], bsb[:, c:c + 1],
                    ALU.mult, ALU.add), inc="sV"))
            st_["hnw"] = (SEM["sV"], hn_done[-1])
            st_["hnl"] = hn_done[-1]
            prev_hn = hn_done[-1]

        def f_w1(G):
            st_ = gstates[G]
            fhn = st_["fhn"]
            mdone = []
            for i in range(NHC):
                wa = 128 if i < 21 else 42
                Ap, Gp = fA[i % 2], fG[i % 2]
                tw = [st_["hnw"]]
                if i < 2 and prev_w2_pair is not None:
                    tw.append((SEM["sV"], prev_w2_pair[i]))
                if len(mdone) >= 2:
                    tw.append((SEM["sV"], mdone[-2]))

                def mma(t, i=i, wa=wa, Ap=Ap, Gp=Gp, fhn=fhn):
                    for c in range(NC):
                        t.matmul(Ap[0:wa, 0:512],
                                 w1s[:, FF * c + 128 * i:FF * c + 128 * i + wa],
                                 fhn[c][:], start=(c == 0), stop=(c == NC - 1))
                    for c in range(NC):
                        ins = t.matmul(Gp[0:wa, 0:512],
                                       w1s[:, FF * c + HID + 128 * i:
                                           FF * c + HID + 128 * i + wa],
                                       fhn[c][:], start=(c == 0), stop=(c == NC - 1))
                    return ins
                mw1 = p.op("tensor", mma, waits=tw, inc="sP")
                if i == 0:
                    st_["w1m0"] = mw1
                cg = p.op("scalar", lambda sc, i=i, wa=wa, Gp=Gp: sc.activation(
                    fga[i % 2][0:wa, :], Gp[0:wa, 0:512], AF.Gelu, bias=b1gs[0:wa, i:i + 1]),
                    waits=[(SEM["sP"], mw1)], inc="sC")
                va = p.op("vector", lambda v, i=i, wa=wa, Ap=Ap: v.tensor_scalar_add(
                    fav[i % 2][0:wa, :], Ap[0:wa, 0:512], b1as[0:wa, i:i + 1]),
                    waits=[(SEM["sP"], mw1)], inc="sV")
                mdone.append(p.op("vector", lambda v, i=i, wa=wa: v.tensor_tensor(
                    fm[0:wa, 512 * i:512 * (i + 1)], fav[i % 2][0:wa, :],
                    fga[i % 2][0:wa, :], ALU.mult),
                    waits=[(SEM["sC"], cg)], inc="sV"))
            st_["mw"] = (SEM["sV"], mdone[-1])
            st_["w2_done"] = []
            for dc in range(2):
                yw = [(SEM["sV"], st_["hnl"])]
                if prev_w2 is not None:
                    yw.append((SEM["sV"], prev_w2))
                p.op("sync", lambda s, dc=dc, G=G, yT_=yout: s.dma_start(
                    fyt[dc][:], yT_[128 * dc:128 * (dc + 1), 512 * G:512 * (G + 1)]),
                    waits=yw, inc="qY", inck=16)

        def f_w2(G, dclo, dchi):
            nonlocal prev_w2, prev_w2_pair
            st_ = gstates[G]
            w2_done = st_["w2_done"]
            for dc in range(dclo, dchi):
                W2 = fA[dc % 2]
                tw = [st_["mw"]]
                if len(w2_done) >= 2:
                    tw.append((SEM["sV"], w2_done[-2][0]))

                def mmw2(t, dc=dc, W2=W2):
                    for hc in range(NHC):
                        ins = t.matmul(W2[:, 0:512],
                                       w2s[:, D * hc + 128 * dc:D * hc + 128 * (dc + 1)],
                                       fm[:, 512 * hc:512 * (hc + 1)],
                                       start=(hc == 0), stop=(hc == NHC - 1))
                    return ins
                mo2 = p.op("tensor", mmw2, waits=tw, inc="sP")
                vy2 = p.op("vector", lambda v, dc=dc, W2=W2: v.scalar_tensor_tensor(
                    fy2[dc % 2][:], W2[:, 0:512], b2s[:, dc:dc + 1], fyt[dc % 2][:],
                    ALU.add, ALU.add),
                    waits=[(SEM["sP"], mo2), p.w("qY"), p.w("qS3"), p.w("qHG")], inc="sV")
                if dc + 2 < NC:
                    p.op("sync", lambda s, dc=dc, G=G, yT_=yout: s.dma_start(
                        fyt[dc % 2][:], yT_[128 * (dc + 2):128 * (dc + 3),
                                            512 * G:512 * (G + 1)]),
                        waits=[(SEM["sV"], vy2)], inc="qY", inck=16)
                p.op("sync", lambda s, dc=dc, G=G, yT_=yin: s.dma_start(
                    yT_[128 * dc:128 * (dc + 1), 512 * G:512 * (G + 1)], fy2[dc % 2][:]),
                    waits=[(SEM["sV"], vy2)], inc="qS3", inck=16)
                if l == 0:
                    tw2 = barF("gpsimd") if (G == 0 and dc == 0) else []
                    p.op("gpsimd", lambda g, dc=dc, G=G, hT_=hin: g.dma_start(
                        hT_[128 * dc:128 * (dc + 1), 512 * G:512 * (G + 1)], fy2[dc % 2][:]),
                        waits=tw2 + [(SEM["sV"], vy2)], inc="qHG", inck=16)
                w2_done.append((vy2, mo2))
            if dchi == NC:
                prev_w2 = w2_done[-1][0]
                prev_w2_pair = (w2_done[-2][0], w2_done[-1][0])

        for G in range(8):
            f_loads(G)
            if G > 0:
                f_w2(G - 1, 0, 4)
            f_stats(G)
            if G > 0:
                f_w2(G - 1, 4, 6)
            f_bcast_hn(G)
            if G > 0:
                f_w2(G - 1, 6, 8)
            f_w1(G)
        f_w2(7, 0, 8)


        if debug:
            p.op("sync", lambda s_, l=l: s_.dma_start(d_dbgY[2 * l + 1][:], yin[:]),
                 waits=[p.w("qS3"), p.w("qHG")], inc="qD", inck=16)
        stackF.close()

    for l in range(2):
        _build_layer(l)

    stackC = ExitStack()
    c_a = [stackC.enter_context(nc.sbuf_tensor(f"c_a{i}", [128, D], F32)) for i in range(2)]
    c_b = [stackC.enter_context(nc.sbuf_tensor(f"c_b{i}", [128, D], F32)) for i in range(2)]
    c_s = [stackC.enter_context(nc.sbuf_tensor(f"c_s{i}", [128, D], F32)) for i in range(2)]
    c_o = [stackC.enter_context(nc.sbuf_tensor(f"c_o{i}", [128, D], F32)) for i in range(2)]
    c_pt = [stackC.enter_context(nc.psum_tensor(f"c_pt{i}", [128, 128], F32)) for i in range(2)]

    barC = p.barrier()
    cst = []
    cev = {}
    for i in range(NT):
        b = i % 2
        tw = barC("sync") if i == 0 else []
        if i >= 1:
            tw.append((SEM["sV"], cev[(i - 1, "sum")]))
        if i >= 2:
            tw.append((SEM["sV"], cev[(i - 2, "sum")]))
        p.op("sync", lambda s, i=i, b=b: s.dma_start(
            c_a[b][:].rearrange("q (c f) -> q c f", c=NC),
            d_x1T[:].rearrange("(c q) s_ -> q c s_", q=128)[:, :, 128 * i:128 * (i + 1)]),
            waits=tw, inc="qL", inck=16)
        p.op("sync", lambda s, i=i, b=b: s.dma_start(
            c_b[b][:].rearrange("q (c f) -> q c f", c=NC),
            d_yT[0][:].rearrange("(c q) s_ -> q c s_", q=128)[:, :, 128 * i:128 * (i + 1)]),
            inc="qL", inck=16)
        ldw2 = p.w("qL")
        tw = barC("vector") if i == 0 else []
        if i >= 2:
            tw.append((SEM["sP"], cev[(i - 2, "tp")]))
        vs = p.op("vector", lambda v, b=b: v.tensor_tensor(c_s[b][:], c_a[b][:], c_b[b][:],
                                                           ALU.add),
                  waits=tw + [ldw2], inc="sV")
        cev[(i, "sum")] = vs
        for c in range(NC):
            tw = barC("tensor") if (i == 0 and c == 0) else []
            if (i, c - 2) in cev:
                tw.append((SEM["sV"], cev[(i, c - 2)]))
            elif (i - 1, c + 6) in cev:
                tw.append((SEM["sV"], cev[(i - 1, c + 6)]))
            tp = p.op("tensor", lambda t, b=b, c=c: t.transpose(
                c_pt[c % 2][:], c_s[b][:, 128 * c:128 * (c + 1)], eye_f[:]),
                waits=tw + [(SEM["sV"], vs)], inc="sP")
            if c == NC - 1:
                cev[(i, "tp")] = tp
            ew = [(SEM["sP"], tp)]
            if c == 0 and i >= 2:
                ew.append(p.w("qS4"))
            cev[(i, c)] = p.op("vector", lambda v, b=b, c=c: v.tensor_copy(
                c_o[b][:, 128 * c:128 * (c + 1)], c_pt[c % 2][:]), waits=ew, inc="sV")
        cst.append(p.op("sync", lambda s, i=i, b=b: s.dma_start(
            d_out[128 * i:128 * (i + 1), :], c_o[b][:]),
            waits=[(SEM["sV"], cev[(i, NC - 1)])], inc="qS4", inck=16))

    p.op("sync", None, waits=[p.w(n) for n in
                              ("qS4", "qS3", "qS2", "qS1", "qZ", "qL", "qY", "qWG", "qHG",
                               "qD")])
    stackC.close()
    stack.close()

    p.emit()
    return nc


_NC_CACHE = None


def _host_tables():
    hi = np.arange(0, 64, 2, dtype=np.float32)
    invf = (np.float32(1.0) / np.power(np.float32(10000.0), hi / np.float32(64.0))
            ).astype(np.float32)
    t = np.arange(S, dtype=np.float32)
    freqs = t[:, None] * invf[None, :]
    freqs = np.concatenate([freqs, freqs], axis=1).astype(np.float32)
    base = ((hi + np.float32(0.4 * 64)) / np.float32(1.4 * 64)).astype(np.float32)
    power = ((t - np.float32(S // 2)) / np.float32(256.0)).astype(np.float32)
    scale = np.power(base[None, :], power[:, None]).astype(np.float32)
    scale = np.concatenate([scale, scale], axis=1).astype(np.float32)
    cos = np.cos(freqs).astype(np.float32)
    sin = np.sin(freqs).astype(np.float32)
    return (np.ascontiguousarray(cos * scale), np.ascontiguousarray(sin * scale),
            np.ascontiguousarray(cos / scale), np.ascontiguousarray(sin / scale))


def kernel(L0_attn_x, L0_attn_MhWqkv, L0_ffn_wqkv, L0_ffn_q_scale, L0_ffn_k_scale,
           L0_ffn_wo, L0_ffn_ln_g, L0_ffn_ln_b, L0_ffn_w1, L0_ffn_b1, L0_ffn_w2,
           L0_ffn_b2):
    global _NC_CACHE
    if _NC_CACHE is None:
        _NC_CACHE = build_nc()
    nc = _NC_CACHE

    f32 = lambda a: np.ascontiguousarray(np.asarray(a, dtype=np.float32))
    r = f32(L0_attn_MhWqkv).reshape(D, 16, 3, 64)
    wq = np.ascontiguousarray(r[:, :, 0, :].reshape(D, D))
    wk = np.ascontiguousarray(r[:, :, 1, :].reshape(D, D))
    wv = np.ascontiguousarray(r[:, :, 2, :].reshape(D, D))
    tCS, tSS, tCiS, tSiS = _host_tables()

    shared = {"wq": wq, "wk": wk, "wv": wv,
              "tCS": tCS, "tSS": tSS, "tCiS": tCiS, "tSiS": tSiS}
    for l in range(2):
        shared[f"iwqkv{l}"] = f32(L0_ffn_wqkv[l])
        shared[f"iwo{l}"] = f32(L0_ffn_wo[l])
        shared[f"iw1{l}"] = f32(L0_ffn_w1[l])
        w2p = np.zeros((HIDP, D), dtype=np.float32)
        w2p[:HID] = f32(L0_ffn_w2[l])
        shared[f"iw2{l}"] = w2p
        shared[f"qs{l}"] = f32(L0_ffn_q_scale[l]).reshape(1, 64)
        shared[f"ks{l}"] = f32(L0_ffn_k_scale[l]).reshape(1, 64)
        shared[f"g{l}"] = np.ascontiguousarray(f32(L0_ffn_ln_g[l]).reshape(8, 128).T)
        shared[f"bb{l}"] = np.ascontiguousarray(f32(L0_ffn_ln_b[l]).reshape(8, 128).T)
        b1 = f32(L0_ffn_b1[l])
        b1a = np.zeros(HIDP, dtype=np.float32)
        b1a[:HID] = b1[:HID]
        b1g = np.zeros(HIDP, dtype=np.float32)
        b1g[:HID] = b1[HID:]
        shared[f"b1a{l}"] = np.ascontiguousarray(b1a.reshape(22, 128).T)
        shared[f"b1g{l}"] = np.ascontiguousarray(b1g.reshape(22, 128).T)
        shared[f"b2{l}"] = np.ascontiguousarray(f32(L0_ffn_b2[l]).reshape(8, 128).T)

    xs = f32(L0_attn_x)
    in_maps = []
    for c in range(8):
        m = dict(shared)
        m["x"] = np.ascontiguousarray(xs[c])
        in_maps.append(m)

    res = run_bass_kernel_spmd(nc, in_maps, core_ids=list(range(8)))
    out = np.stack([res.results[c]["out"] for c in range(8)], axis=0)
    return out.astype(np.float32)
